# version 09
# baseline (speedup 1.0000x reference)
import sys
from contextlib import ExitStack

import numpy as np
import ml_dtypes

sys.path.insert(0, "/opt/trn_rl_repo")
import concourse.bass as bass
import concourse.mybir as mybir
from concourse.bass_utils import run_bass_kernel_spmd
from concourse.alu_op_type import AluOpType as OP

F32 = mybir.dt.float32
BF16 = mybir.dt.bfloat16
I32 = mybir.dt.int32
AF = mybir.ActivationFunctionType
AX = mybir.AxisListType

D = 2048
S = 2048
NDC = D // 128
TSL = 1024
NSL = S // TSL
HID = 5632
NHC = HID // 128
NH = 16
NKV = 4
HD = 128
WIN = 512
SCALE_A = 128.0 ** -0.5
LN_SCALE_A = float(np.log(SCALE_A))

ENGINES = ("sync", "gpsimd", "tensor", "vector", "scalar")

DBG = {"enable": False}


class Em:

    def __init__(self, nc):
        self.nc = nc
        self.ops = {e: [] for e in ENGINES}
        self.sems = {}
        self.cnt = {}
        self.waited = {e: {} for e in ENGINES}

    def add_sem(self, name, handle):
        self.sems[name] = handle
        self.cnt[name] = 0

    def instr(self, eng, method, *args, waits=(), inc=None, incn=1, **kw):
        ws = []
        for wv in waits:
            if wv is None:
                continue
            s, v = wv
            if v is None or v <= 0:
                continue
            if self.waited[eng].get(s, 0) >= v:
                continue
            ws.append((s, v))
            self.waited[eng][s] = v
        marker = None
        if inc is not None:
            self.cnt[inc] += incn
            marker = (inc, self.cnt[inc])
        sems = self.sems

        def fn(e, ws=tuple(ws), method=method, args=args, kw=kw, inc=inc, incn=incn):
            for s, v in ws:
                e.wait_ge(sems[s], v)
            ins = getattr(e, method)(*args, **kw)
            if inc is not None:
                ins.then_inc(sems[inc], incn)

        self.ops[eng].append(fn)
        return marker

    def wait(self, eng, *waits):
        ws = []
        for wv in waits:
            if wv is None:
                continue
            s, v = wv
            if v is None or v <= 0:
                continue
            if self.waited[eng].get(s, 0) >= v:
                continue
            ws.append((s, v))
            self.waited[eng][s] = v
        if not ws:
            return
        sems = self.sems

        def fn(e, ws=tuple(ws)):
            for s, v in ws:
                e.wait_ge(sems[s], v)

        self.ops[eng].append(fn)

    def last(self, sem):
        return (sem, self.cnt[sem])


class PsumPool:

    def __init__(self, tensor_handle, nbanks):
        self.t = tensor_handle
        self.nbanks = nbanks
        self.readers = [[] for _ in range(nbanks)]
        self.ptr = 0

    def alloc(self, nb):
        if self.ptr + nb > self.nbanks:
            self.ptr = 0
        start = self.ptr
        self.ptr += nb
        waits = []
        for b in range(start, start + nb):
            waits.extend(self.readers[b])
            self.readers[b] = []
        ap = self.t[:, start * 512 : (start + nb) * 512]
        return ap, waits, (start, nb)

    def add_reader(self, slot, marker):
        start, nb = slot
        for b in range(start, start + nb):
            self.readers[b].append(marker)


def build_program(n_phases=4):
    nc = bass.Bass("TRN2", target_bir_lowering=False)

    xT = nc.dram_tensor("xT", [NDC, 128, S], F32, kind="ExternalInput").ap()
    wqa = nc.dram_tensor("wqa", [NDC, NDC, 128, 128], F32, kind="ExternalInput").ap()
    wka = nc.dram_tensor("wka", [NDC, NDC, 128, 128], F32, kind="ExternalInput").ap()
    wva = nc.dram_tensor("wva", [NDC, NDC, 128, 128], F32, kind="ExternalInput").ap()
    woa = nc.dram_tensor("woa", [NDC, NDC, 128, 128], F32, kind="ExternalInput").ap()
    biases = nc.dram_tensor("biases", [128, 64], F32, kind="ExternalInput").ap()
    wg = nc.dram_tensor("wg", [NHC, NDC, 128, 128], I32, kind="ExternalInput").ap()
    wu = nc.dram_tensor("wu", [NHC, NDC, 128, 128], I32, kind="ExternalInput").ap()
    wd = nc.dram_tensor("wd", [NDC, NHC, 128, 128], I32, kind="ExternalInput").ap()
    sgu = nc.dram_tensor("sgu", [128, 2 * NHC], F32, kind="ExternalInput").ap()
    sdn = nc.dram_tensor("sdn", [128, NDC], F32, kind="ExternalInput").ap()
    wqb = nc.dram_tensor("wqb", [NH, NDC, 128, 128], F32, kind="ExternalInput").ap()
    wkb = nc.dram_tensor("wkb", [NKV, NDC, 128, 128], F32, kind="ExternalInput").ap()
    wvb = nc.dram_tensor("wvb", [NKV, NDC, 128, 128], F32, kind="ExternalInput").ap()
    wob = nc.dram_tensor("wob", [NDC, NH, 128, 128], F32, kind="ExternalInput").ap()
    lnw = nc.dram_tensor("lnw", [128, NDC], F32, kind="ExternalInput").ap()
    ones_c = nc.dram_tensor("ones_c", [128, 128], BF16, kind="ExternalInput").ap()
    ident_c = nc.dram_tensor("ident_c", [128, 128], BF16, kind="ExternalInput").ap()
    neg128_c = nc.dram_tensor("neg128_c", [1, 128], BF16, kind="ExternalInput").ap()
    maske_c = nc.dram_tensor("maske_c", [128, 1280], BF16, kind="ExternalInput").ap()
    maskb_c = nc.dram_tensor("maskb_c", [128, 640], BF16, kind="ExternalInput").ap()
    cos2_c = nc.dram_tensor("cos2_c", [128, S], BF16, kind="ExternalInput").ap()
    sin2_c = nc.dram_tensor("sin2_c", [128, S], BF16, kind="ExternalInput").ap()
    y = nc.dram_tensor("y", [NDC, 128, S], F32, kind="ExternalOutput").ap()
    if DBG["enable"]:
        ydbg = nc.dram_tensor("ydbg", [3, 8, 128, 640], F32, kind="ExternalOutput").ap()
        aodbg = nc.dram_tensor("aodbg", [128, NH * TSL], F32, kind="ExternalOutput").ap()

    st = ExitStack()
    E = st.enter_context
    em = Em(nc)
    for sname in ("dX", "dC", "oA", "oB", "pe", "dv", "ac", "gp",
                  "w0", "w1", "w2", "w3", "w4", "w5", "dD"):
        em.add_sem(sname, E(nc.semaphore(sname)))

    x_sb = E(nc.sbuf_tensor("x_sb", [128, NDC * S], BF16))
    ones_sb = E(nc.sbuf_tensor("ones_sb", [128, 128], BF16))
    ident_sb = E(nc.sbuf_tensor("ident_sb", [128, 128], BF16))
    neg128_sb = E(nc.sbuf_tensor("neg128_sb", [1, 128], BF16))
    b_sb = E(nc.sbuf_tensor("b_sb", [128, 64], F32))
    bqs_sb = E(nc.sbuf_tensor("bqs_sb", [128, 16], F32))
    sgu_sb = E(nc.sbuf_tensor("sgu_sb", [128, 2 * NHC], F32))
    sdn_sb = E(nc.sbuf_tensor("sdn_sb", [128, NDC], F32))
    lnw_sb = E(nc.sbuf_tensor("lnw_sb", [128, NDC], F32))

    def xc(dc, sl=None):
        if sl is None:
            return x_sb[:, dc * S : (dc + 1) * S]
        return x_sb[:, dc * S + sl * TSL : dc * S + sl * TSL + TSL]

    for dc in range(NDC):
        em.instr("gpsimd", "dma_start", x_sb[:, dc * S : (dc + 1) * S], xT[dc, :, :],
                 inc="dX", incn=16)
    x_loaded = em.last("dX")
    for dst, src in ((ones_sb, ones_c), (ident_sb, ident_c), (neg128_sb, neg128_c),
                     (b_sb, biases), (sgu_sb, sgu), (sdn_sb, sdn), (lnw_sb, lnw)):
        em.instr("sync", "dma_start", dst[:], src, inc="dC", incn=16)
    consts_loaded = em.last("dC")
    em.instr("vector", "tensor_scalar_mul", bqs_sb[:, :], b_sb[:, 0:16], SCALE_A,
             waits=[consts_loaded], inc="dv")
    constv_sb = E(nc.sbuf_tensor("constv_sb", [128, 4], F32))
    for j, v in enumerate((1e-5, 1e-11, LN_SCALE_A)):
        em.instr("vector", "memset", constv_sb[:, j : j + 1], v, inc="dv")
        nc.const_aps.register(F32, v, constv_sb[:, j : j + 1])
    consts_ready = em.last("dv")
    em.wait("tensor", consts_loaded)
    em.wait("scalar", consts_ready)

    def emit_ln(sl, xn_sb, pool, kind, stat, xsq_sb, x_ready):
        ps_sum, w1, slot1 = pool.alloc(2)
        ps_sq, w2, slot2 = pool.alloc(2)
        sq_mk = [None] * NDC
        for dc in range(NDC):
            pp = xsq_sb[:, (dc % 2) * TSL : (dc % 2 + 1) * TSL]
            war = sq_mk[dc - 2] if dc >= 2 else None
            em.instr("vector", "tensor_mul", pp, xc(dc, sl), xc(dc, sl),
                     waits=[x_ready, war], inc="dv")
            sq_ready = em.last("dv")
            for nh in range(2):
                cols = slice(nh * 512, nh * 512 + 512)
                em.instr("tensor", "matmul", ps_sum[:, cols], ones_sb[:, :],
                         xc(dc, sl)[:, cols], start=(dc == 0), stop=(dc == NDC - 1),
                         waits=[x_ready] + (w1 if dc == 0 else []))
                sq_mk[dc] = em.instr("tensor", "matmul", ps_sq[:, cols], ones_sb[:, :],
                                     pp[:, cols], start=(dc == 0), stop=(dc == NDC - 1),
                                     waits=[sq_ready] + (w2 if dc == 0 else []),
                                     inc="pe")
        sums_done = em.last("pe")
        m_b, c_b, t_b, u_b = stat["m"], stat["c"], stat["t"], stat["u"]
        em.instr("vector", "tensor_scalar_mul", m_b[:], ps_sum[:, 0:TSL], 1.0 / D,
                 waits=[sums_done], inc="dv")
        em.instr("vector", "tensor_scalar_mul", t_b[:], ps_sq[:, 0:TSL], 1.0 / D, inc="dv")
        em.instr("vector", "tensor_mul", u_b[:], m_b[:], m_b[:], inc="dv")
        em.instr("vector", "tensor_sub", t_b[:], t_b[:], u_b[:], inc="dv")
        var_done = em.last("dv")
        pool.add_reader(slot1, var_done)
        pool.add_reader(slot2, var_done)
        if kind == "attnb":
            a_, b_ = 1.000001, 1e-11
        else:
            a_, b_ = 1.0, 1e-5
        em.instr("scalar", "activation", t_b[:], t_b[:], AF.Ln, scale=a_, bias=b_,
                 waits=[var_done], inc="ac")
        em.instr("scalar", "activation", c_b[:], t_b[:], AF.Exp, scale=-0.5, inc="ac")
        c_done = em.last("ac")
        gp_scr = xsq_sb[:, 0:TSL]
        for dc in range(NDC):
            xn = xn_sb[:, dc * TSL : (dc + 1) * TSL]
            if dc % 2 == 0:
                em.instr("vector", "tensor_sub", u_b[:], xc(dc, sl), m_b[:],
                         waits=[c_done], inc="dv")
                if kind == "attnb":
                    em.instr("vector", "tensor_mul", u_b[:], u_b[:], c_b[:], inc="dv")
                    em.instr("vector", "tensor_scalar_mul", xn, u_b[:],
                             lnw_sb[:, dc : dc + 1], inc="dv")
                else:
                    em.instr("vector", "tensor_mul", xn, u_b[:], c_b[:], inc="dv")
            else:
                em.instr("gpsimd", "tensor_tensor", gp_scr, xc(dc, sl), m_b[:],
                         OP.subtract, waits=[c_done, sums_done], inc="gp")
                if kind == "attnb":
                    em.instr("gpsimd", "tensor_mul", gp_scr, gp_scr, c_b[:], inc="gp")
                    em.instr("gpsimd", "tensor_scalar_mul", xn, gp_scr,
                             lnw_sb[:, dc : dc + 1], inc="gp")
                else:
                    em.instr("gpsimd", "tensor_mul", xn, gp_scr, c_b[:], inc="gp")
        em.instr("vector", "nop", waits=[em.last("gp")], inc="dv")
        return em.last("dv")

    class WStream:

        def __init__(self, buf, nslots, blkcols, sems):
            self.buf = buf
            self.nslots = nslots
            self.blkcols = blkcols
            self.readers = [None] * nslots
            self.sems = sems
            self.i = 0

        def slot_ap(self, i):
            c = (i % self.nslots) * self.blkcols
            return self.buf[:, c : c + self.blkcols]

        def load(self, src_tiled, blk, nchunks):
            return self.load_ap(src_tiled[blk], nchunks)

        def load_ap(self, ap3d, nchunks):
            i = self.i
            self.i += 1
            sem = self.sems[i % self.nslots]
            em.instr("gpsimd", "dma_start",
                     self.slot_ap(i).rearrange("p (c q) -> p c q", c=nchunks),
                     ap3d.rearrange("c p q -> p c q"),
                     waits=[self.readers[i % self.nslots]], inc=sem, incn=16)
            return i, em.last(sem)

        def set_reader(self, i, marker):
            self.readers[i % self.nslots] = marker

    def emit_attn_a(x_ready):
        with ExitStack() as ph:
            P = ph.enter_context
            xn_sb = P(nc.sbuf_tensor("a_xn", [128, NDC * TSL], BF16))
            ctx_sb = P(nc.sbuf_tensor("a_ctx", [128, NDC * TSL], BF16))
            phiq_sb = P(nc.sbuf_tensor("a_phiq", [128, TSL], BF16))
            scrA = P(nc.sbuf_tensor("a_scrA", [128, TSL], F32))
            scrB = P(nc.sbuf_tensor("a_scrB", [128, TSL], F32))
            scrC = P(nc.sbuf_tensor("a_scrC", [128, TSL], F32))
            zeros_sb = P(nc.sbuf_tensor("a_zeros", [128, TSL], F32))
            carry_sb = P(nc.sbuf_tensor("a_carry", [128, NDC], F32))
            xsq_sb = P(nc.sbuf_tensor("a_xsq", [128, 2 * TSL], BF16))
            stat = {k: P(nc.sbuf_tensor("a_st_" + k, [128, TSL],
                                        F32 if k == "t" else BF16))
                    for k in ("m", "c", "t", "u")}
            wbuf = P(nc.sbuf_tensor("a_wbuf", [128, 6 * NDC * 128], BF16))
            psum = P(nc.psum_tensor("a_psum", [128, 8 * 512], F32))
            pool = PsumPool(psum, 8)
            ws = WStream(wbuf, 6, NDC * 128, ["w0", "w1", "w2", "w3", "w4", "w5"])

            em.instr("vector", "memset", zeros_sb[:], 0.0, waits=[consts_ready], inc="dv")
            em.instr("gpsimd", "memset", carry_sb[:], 0.0, inc="gp")
            carry_mark = [em.last("gp")] * NDC

            pending = {}

            def prefetch(sl, ec):
                for name, src in (("q", wqa), ("k", wka), ("v", wva)):
                    pending[(sl, ec, name)] = ws.load(src, ec, NDC)

            prefetch(0, 0)
            res_mark = x_ready
            for sl in range(NSL):
                xn_ready = emit_ln(sl, xn_sb, pool, "plain", stat, xsq_sb, res_mark)
                last_ctx = None
                for ec in range(NDC):
                    if ec + 1 < NDC:
                        prefetch(sl, ec + 1)
                    ps = {}
                    for name in ("q", "k", "v"):
                        i, loaded = pending.pop((sl, ec, name))
                        pp, wv, slot = pool.alloc(2)
                        for nh in range(2):
                            for dc in range(NDC):
                                mk = em.instr(
                                    "tensor", "matmul",
                                    pp[:, nh * 512 : nh * 512 + 512],
                                    ws.slot_ap(i)[:, dc * 128 : (dc + 1) * 128],
                                    xn_sb[:, dc * TSL + nh * 512 : dc * TSL + nh * 512 + 512],
                                    start=(dc == 0), stop=(dc == NDC - 1),
                                    waits=([loaded, xn_ready] + wv) if (nh == 0 and dc == 0) else (),
                                    inc="pe" if (nh == 1 and dc == NDC - 1) else None,
                                )
                        ws.set_reader(i, mk)
                        ps[name] = (pp, slot, mk)
                    ppq, slotq, mmq = ps["q"]
                    ppk, slotk, mmk = ps["k"]
                    ppv, slotv, mmv = ps["v"]
                    bq_ap = b_sb[:, ec : ec + 1]
                    bqs_ap = bqs_sb[:, ec : ec + 1]
                    bk_ap = b_sb[:, 16 + ec : 16 + ec + 1]
                    bv_ap = b_sb[:, 32 + ec : 32 + ec + 1]
                    em.instr("scalar", "activation", scrA[:], ppq[:, 0:TSL], AF.Relu,
                             scale=SCALE_A, bias=bqs_ap, waits=[mmq, em.last("dv")],
                             inc="ac")
                    rq_done = em.last("ac")
                    em.instr("vector", "tensor_scalar", scrB[:], ppq[:, 0:TSL],
                             bq_ap, 0.0, OP.add, OP.min, waits=[mmq, rq_done], inc="dv")
                    mnq_done = em.last("dv")
                    pool.add_reader(slotq, mnq_done)
                    em.instr("scalar", "activation", scrC[:], scrB[:], AF.Exp,
                             bias=LN_SCALE_A, waits=[mnq_done], inc="ac")
                    eq_done = em.last("ac")
                    em.instr("vector", "tensor_add", phiq_sb[:], scrA[:], scrC[:],
                             waits=[eq_done], inc="dv")
                    em.instr("scalar", "activation", scrA[:], ppk[:, 0:TSL], AF.Relu,
                             bias=bk_ap, waits=[mmk, em.last("dv")], inc="ac")
                    rk_done = em.last("ac")
                    em.instr("vector", "tensor_scalar", scrB[:], ppk[:, 0:TSL],
                             bk_ap, 0.0, OP.add, OP.min, waits=[mmk, rk_done], inc="dv")
                    mnk_done = em.last("dv")
                    pool.add_reader(slotk, mnk_done)
                    em.instr("scalar", "activation", scrC[:], scrB[:], AF.Exp,
                             waits=[mnk_done], inc="ac")
                    ek_done = em.last("ac")
                    em.instr("vector", "tensor_add", scrA[:], scrA[:], scrC[:],
                             waits=[ek_done], inc="dv")
                    em.instr("vector", "scalar_tensor_tensor", scrB[:], ppv[:, 0:TSL],
                             bv_ap, scrA[:], OP.add, OP.mult, waits=[mmv], inc="dv")
                    pool.add_reader(slotv, em.last("dv"))
                    em.instr("vector", "tensor_tensor_scan", scrC[:], scrB[:], zeros_sb[:],
                             carry_sb[:, ec : ec + 1], OP.add, OP.add,
                             waits=[carry_mark[ec]], inc="dv")
                    scan_done = em.last("dv")
                    if sl + 1 < NSL:
                        em.instr("scalar", "copy", carry_sb[:, ec : ec + 1],
                                 scrC[:, TSL - 1 : TSL], waits=[scan_done], inc="ac")
                        carry_mark[ec] = em.last("ac")
                    em.instr("vector", "tensor_mul",
                             ctx_sb[:, ec * TSL : (ec + 1) * TSL], phiq_sb[:], scrC[:],
                             inc="dv")
                    last_ctx = em.last("dv")
                wo_pending = {}
                wo_pending[0] = ws.load(woa, 0, NDC)
                wo_pending[1] = ws.load(woa, 1, NDC)
                for eo in range(NDC):
                    if eo + 2 < NDC:
                        wo_pending[eo + 2] = ws.load(woa, eo + 2, NDC)
                    elif eo == NDC - 2 and sl + 1 < NSL:
                        prefetch(sl + 1, 0)
                    i, loaded = wo_pending.pop(eo)
                    pp, wv, slot = pool.alloc(2)
                    for nh in range(2):
                        for ec in range(NDC):
                            mk = em.instr(
                                "tensor", "matmul",
                                pp[:, nh * 512 : nh * 512 + 512],
                                ws.slot_ap(i)[:, ec * 128 : (ec + 1) * 128],
                                ctx_sb[:, ec * TSL + nh * 512 : ec * TSL + nh * 512 + 512],
                                start=(ec == 0), stop=(ec == NDC - 1),
                                waits=([loaded, last_ctx] + wv) if (nh == 0 and ec == 0) else (),
                                inc="pe" if (nh == 1 and ec == NDC - 1) else None,
                            )
                    ws.set_reader(i, mk)
                    bo_ap = b_sb[:, 48 + eo : 48 + eo + 1]
                    em.instr("vector", "scalar_tensor_tensor", xc(eo, sl),
                             pp[:, 0:TSL], bo_ap, xc(eo, sl), OP.add, OP.add,
                             waits=[mk], inc="dv")
                    pool.add_reader(slot, em.last("dv"))
                res_mark = em.last("dv")
        return res_mark

    def emit_ffn(x_ready, final, tag):
        with ExitStack() as ph:
            P = ph.enter_context
            psum = P(nc.psum_tensor(tag + "_psum", [128, 8 * 512], F32))
            pool = PsumPool(psum, 6)
            cs_ps = psum[:, 6 * 512 : 8 * 512]
            res_mark = x_ready
            gu_done = None
            cs_reader = None
            stage_mk = [None, None]
            for sl in range(NSL):
                with ExitStack() as slsc:
                    Q = slsc.enter_context
                    xnsc = ExitStack()
                    X_ = xnsc.enter_context
                    xn_sb = X_(nc.sbuf_tensor(f"{tag}{sl}_xn", [128, NDC * TSL], BF16))
                    with ExitStack() as lnsc:
                        L = lnsc.enter_context
                        stat = {k: L(nc.sbuf_tensor(f"{tag}{sl}_st_{k}", [128, TSL],
                                            F32 if k == "t" else BF16))
                                for k in ("m", "c", "t", "u")}
                        xsq_sb = L(nc.sbuf_tensor(f"{tag}{sl}_xsq", [128, 2 * TSL], BF16))
                        xn_ready = emit_ln(sl, xn_sb, pool, "plain", stat, xsq_sb,
                                           res_mark)
                    act_sb = Q(nc.sbuf_tensor(f"{tag}{sl}_act", [128, NHC * TSL], BF16,
                                              side="right"))
                    with ExitStack() as gusc:
                        G = gusc.enter_context
                        gu_buf = G(nc.sbuf_tensor(f"{tag}{sl}_gu", [128, 6 * 8 * 128], BF16))
                        g_scr = G(nc.sbuf_tensor(f"{tag}{sl}_g", [128, 2 * TSL], BF16))
                        gws = WStream(gu_buf, 6, 8 * 128,
                                      ["w0", "w1", "w2", "w3", "w4", "w5"])
                        gws.readers = [xn_ready] * 6
                        gu_jobs = []
                        for hc_ in range(NHC):
                            for src_ in (wg, wu):
                                for hf in range(2):
                                    gu_jobs.append(src_[hc_, hf * 8 : hf * 8 + 8])
                        gu_pend = {}
                        gu_next = [0]

                        def gu_load():
                            j = gu_next[0]
                            gu_next[0] += 1
                            gu_pend[j] = gws.load_ap(gu_jobs[j], 8)

                        for _ in range(6):
                            gu_load()
                        act_mk = {}
                        g_war = [None, None]

                        def emit_cs_act(hcl):
                            for nh in range(2):
                                em.instr("tensor", "matmul",
                                         cs_ps[:, nh * 512 : nh * 512 + 512],
                                         ones_sb[:, :],
                                         act_sb[:, hcl * TSL + nh * 512 :
                                                hcl * TSL + nh * 512 + 512],
                                         start=(hcl == 0), stop=(hcl == NHC - 1),
                                         waits=[act_mk[hcl], cs_reader]
                                         if hcl == 0 else [act_mk[hcl]],
                                         inc="pe" if (hcl == NHC - 1 and nh == 1) else None)

                        for hc in range(NHC):
                            pps = {}
                            for nmi, name in enumerate(("g", "u")):
                                j0 = (hc * 2 + nmi) * 2
                                i0, ld0 = gu_pend.pop(j0)
                                i1, ld1 = gu_pend.pop(j0 + 1)
                                pp, wv, slot = pool.alloc(2)
                                for nh in range(2):
                                    for dc in range(NDC):
                                        ii = i0 if dc < 8 else i1
                                        mk = em.instr(
                                            "tensor", "matmul",
                                            pp[:, nh * 512 : nh * 512 + 512],
                                            gws.slot_ap(ii)[:, (dc % 8) * 128 : (dc % 8 + 1) * 128],
                                            xn_sb[:, dc * TSL + nh * 512 : dc * TSL + nh * 512 + 512],
                                            start=(dc == 0), stop=(dc == NDC - 1),
                                            waits=([ld0, ld1, xn_ready] + wv)
                                            if (nh == 0 and dc == 0) else (),
                                            inc="pe" if (nh == 1 and dc == NDC - 1) else None,
                                        )
                                ws_mk = mk
                                gws.set_reader(i0, ws_mk)
                                gws.set_reader(i1, ws_mk)
                                pps[name] = (pp, slot, ws_mk)
                                for _ in range(2):
                                    if gu_next[0] < len(gu_jobs):
                                        gu_load()
                            ppg, slotg, mmg = pps["g"]
                            ppu, slotu, mmu = pps["u"]
                            gp = g_scr[:, (hc % 2) * TSL : (hc % 2 + 1) * TSL]
                            em.instr("scalar", "activation", gp, ppg[:, 0:TSL],
                                     AF.Silu, scale=sgu_sb[:, hc : hc + 1],
                                     waits=[mmg, g_war[hc % 2]], inc="ac")
                            g_done = em.last("ac")
                            pool.add_reader(slotg, g_done)
                            em.instr("vector", "scalar_tensor_tensor",
                                     act_sb[:, hc * TSL : (hc + 1) * TSL],
                                     gp, sgu_sb[:, NHC + hc : NHC + hc + 1],
                                     ppu[:, 0:TSL], OP.mult, OP.mult,
                                     waits=[mmu, g_done], inc="dv")
                            act_mk[hc] = g_war[hc % 2] = em.last("dv")
                            pool.add_reader(slotu, act_mk[hc])
                            if hc >= 2:
                                emit_cs_act(hc - 2)
                        for hcl in (NHC - 2, NHC - 1):
                            emit_cs_act(hcl)
                        gu_done = em.last("pe")
                    xnsc.close()
                    with ExitStack() as dnsc:
                        DN = dnsc.enter_context
                        dn_buf = DN(nc.sbuf_tensor(f"{tag}{sl}_dn", [128, 2 * NHC * 128], BF16))
                        ca_row = DN(nc.sbuf_tensor(f"{tag}{sl}_carow", [1, TSL], BF16))
                        if final:
                            stage = DN(nc.sbuf_tensor(f"{tag}{sl}_stage", [128, 2 * TSL], F32))
                        dws = WStream(dn_buf, 2, NHC * 128, ["w2", "w3"])
                        dws.readers = [gu_done, gu_done]
                        em.instr("vector", "tensor_copy", ca_row[0:1, :],
                                 cs_ps[0:1, 0:TSL], waits=[gu_done], inc="dv")
                        ca_done = em.last("dv")
                        cs_reader = ca_done
                        dn_pending = {0: dws.load(wd, 0, NHC)}
                        for dout in range(NDC):
                            if dout + 1 < NDC:
                                dn_pending[dout + 1] = dws.load(wd, dout + 1, NHC)
                            i, loaded = dn_pending.pop(dout)
                            pp, wv, slot = pool.alloc(2)
                            for nh in range(2):
                                for hc in range(NHC):
                                    em.instr(
                                        "tensor", "matmul",
                                        pp[:, nh * 512 : nh * 512 + 512],
                                        dws.slot_ap(i)[:, hc * 128 : (hc + 1) * 128],
                                        act_sb[:, hc * TSL + nh * 512 : hc * TSL + nh * 512 + 512],
                                        start=(hc == 0), stop=False,
                                        waits=([loaded] + wv) if (nh == 0 and hc == 0) else (),
                                    )
                                mk = em.instr(
                                    "tensor", "matmul",
                                    pp[:, nh * 512 : nh * 512 + 512],
                                    neg128_sb[0:1, :],
                                    ca_row[0:1, nh * 512 : nh * 512 + 512],
                                    start=False, stop=True,
                                    waits=[ca_done] if nh == 0 else (),
                                    inc="pe" if nh == 1 else None,
                                )
                            dws.set_reader(i, mk)
                            sd_ap = sdn_sb[:, dout : dout + 1]
                            if final:
                                sslot = stage[:, (dout % 2) * TSL : (dout % 2 + 1) * TSL]
                                em.instr("vector", "scalar_tensor_tensor", sslot,
                                         pp[:, 0:TSL], sd_ap, xc(dout, sl),
                                         OP.mult, OP.add,
                                         waits=[mk, stage_mk[dout % 2]], inc="dv")
                                pool.add_reader(slot, em.last("dv"))
                                osem = "oA" if dout % 2 == 0 else "oB"
                                em.instr("sync", "dma_start",
                                         y[dout, :, sl * TSL : (sl + 1) * TSL], sslot,
                                         waits=[em.last("dv")], inc=osem, incn=16)
                                stage_mk[dout % 2] = em.last(osem)
                            else:
                                em.instr("vector", "scalar_tensor_tensor",
                                         xc(dout, sl), pp[:, 0:TSL], sd_ap,
                                         xc(dout, sl), OP.mult, OP.add,
                                         waits=[mk], inc="dv")
                                pool.add_reader(slot, em.last("dv"))
                        res_mark = em.last("dv")
            if final:
                em.wait("sync", em.last("oA"), em.last("oB"))
            return res_mark

    def emit_attn_b(x_ready):
        with ExitStack() as ph:
            P = ph.enter_context
            maske_sb = P(nc.sbuf_tensor("b_maske", [128, 1280], BF16))
            maskb_sb = P(nc.sbuf_tensor("b_maskb", [128, 640], BF16))
            cos_sb = P(nc.sbuf_tensor("b_cos", [128, TSL], BF16))
            sin_sb = P(nc.sbuf_tensor("b_sin", [128, TSL], BF16))
            krot_sb = P(nc.sbuf_tensor("b_krot", [128, NKV * S], BF16))
            vtm_sb = P(nc.sbuf_tensor("b_vtm", [128, NKV * S], BF16))
            wsbuf = P(nc.sbuf_tensor("b_wsbuf", [128, 2 * NDC * 128], BF16))
            qrot_sb = P(nc.sbuf_tensor("b_qrot", [128, 4 * TSL], BF16))
            vbf_sb = P(nc.sbuf_tensor("b_vbf", [128, TSL], BF16))
            u1_sb = P(nc.sbuf_tensor("b_u1", [128, TSL], BF16))
            u2_sb = P(nc.sbuf_tensor("b_u2", [128, TSL], BF16))
            u2s_sb = P(nc.sbuf_tensor("b_u2s", [128, TSL], BF16))
            sc_sb = P(nc.sbuf_tensor("b_sc", [128, 2 * 640], F32))
            attn_sb = P(nc.sbuf_tensor("b_attn", [128, 2 * 640], BF16))
            attn2_sb = P(nc.sbuf_tensor("b_attn2", [128, 2 * 640], BF16))
            attnT_sb = P(nc.sbuf_tensor("b_attnT", [128, 2 * 640], BF16))
            nmax_sb = P(nc.sbuf_tensor("b_nmax", [128, 2], F32))
            rs_sb = P(nc.sbuf_tensor("b_rs", [128, 2], F32))
            rden_sb = P(nc.sbuf_tensor("b_rden", [128, 2], F32))
            psum = P(nc.psum_tensor("b_psum", [128, 6 * 512], F32))
            ps_tr = P(nc.psum_tensor("b_pstr", [128, 2048], BF16))
            pool = PsumPool(psum, 6)

            em.instr("sync", "dma_start", maske_sb[:], maske_c, waits=[consts_ready],
                     inc="dC", incn=16)
            em.instr("sync", "dma_start", maskb_sb[:], maskb_c, inc="dC", incn=16)
            masks_ok = em.last("dC")
            em.wait("vector", masks_ok)
            em.wait("tensor", masks_ok)

            ws = WStream(wsbuf, 2, NDC * 128, ["w0", "w1"])
            qr_mk = [None] * 4
            last_sc = [None] * 4
            ao_last = [None]
            res_mark = x_ready
            rope_done = None
            trev_mk = [None, None]
            vbf_reader = None

            for sl in range(NSL):
                em.instr("sync", "dma_start", cos_sb[:],
                         cos2_c[:, sl * TSL : (sl + 1) * TSL],
                         waits=[rope_done or consts_ready], inc="dC", incn=16)
                em.instr("sync", "dma_start", sin_sb[:],
                         sin2_c[:, sl * TSL : (sl + 1) * TSL], inc="dC", incn=16)
                cs_ok = em.last("dC")
                jobs = []
                for kvh in range(NKV):
                    jobs.append((wkb, kvh))
                    jobs.append((wvb, kvh))
                for h in range(NH):
                    jobs.append((wqb, h))
                for eo in range(NDC):
                    jobs.append((wob, eo))
                pend = {}
                for j in range(2):
                    pend[j] = ws.load(jobs[j][0], jobs[j][1], NDC)
                nextload = [2]

                def consume(j):
                    i, loaded = pend.pop(j)
                    return i, loaded

                def after_use(i, mk):
                    ws.set_reader(i, mk)
                    if nextload[0] < len(jobs):
                        j = nextload[0]
                        nextload[0] += 1
                        pend[j] = ws.load(jobs[j][0], jobs[j][1], NDC)

                with ExitStack() as slsc:
                    Q = slsc.enter_context
                    xn_sb = Q(nc.sbuf_tensor(f"b{sl}_xn", [128, NDC * TSL], BF16))
                    with ExitStack() as lnsc:
                        L = lnsc.enter_context
                        stat = {k: L(nc.sbuf_tensor(f"b{sl}_st_{k}", [128, TSL],
                                            F32 if k == "t" else BF16))
                                for k in ("m", "c", "t", "u")}
                        xsq_sb = L(nc.sbuf_tensor(f"b{sl}_xsq", [128, 2 * TSL], BF16))
                        xn_ready = emit_ln(sl, xn_sb, pool, "attnb", stat, xsq_sb,
                                           res_mark)
                    attnout_sb = Q(nc.sbuf_tensor(f"b{sl}_ao", [128, NH * TSL], BF16,
                                                  side="right"))

                    def proj_1024(jidx, extra_waits):
                        i, loaded = consume(jidx)
                        pp, wv, slot = pool.alloc(2)
                        for nh in range(2):
                            for dc in range(NDC):
                                mk = em.instr(
                                    "tensor", "matmul",
                                    pp[:, nh * 512 : nh * 512 + 512],
                                    ws.slot_ap(i)[:, dc * 128 : (dc + 1) * 128],
                                    xn_sb[:, dc * TSL + nh * 512 : dc * TSL + nh * 512 + 512],
                                    start=(dc == 0), stop=(dc == NDC - 1),
                                    waits=([loaded, xn_ready] + wv + extra_waits)
                                    if (nh == 0 and dc == 0) else (),
                                    inc="pe" if (nh == 1 and dc == NDC - 1) else None,
                                )
                        return pp, slot, mk, i

                    def rope(pp, dst_ap, mm_mk, war=()):
                        em.instr("vector", "tensor_mul", u1_sb[:], pp[:, 0:TSL],
                                 cos_sb[:], waits=[mm_mk, cs_ok], inc="dv")
                        em.instr("vector", "tensor_mul", u2_sb[:], pp[:, 0:TSL],
                                 sin_sb[:], inc="dv")
                        em.instr("vector", "tensor_copy", u2s_sb[0:64, :],
                                 u2_sb[64:128, :], inc="dv")
                        em.instr("vector", "tensor_copy", u2s_sb[64:128, :],
                                 u2_sb[0:64, :], inc="dv")
                        em.instr("vector", "tensor_add", dst_ap[:, :],
                                 u1_sb[:], u2s_sb[:], waits=list(war), inc="dv")
                        return em.last("dv")

                    for kvh in range(NKV):
                        ppk, slotk, mmk, ik = proj_1024(2 * kvh, [])
                        kdst = krot_sb[:, kvh * S + sl * TSL : kvh * S + (sl + 1) * TSL]
                        kr = rope(ppk, kdst, mmk)
                        pool.add_reader(slotk, kr)
                        after_use(ik, kr)
                        ppv, slotv, mmv, iv = proj_1024(2 * kvh + 1, [])
                        em.instr("scalar", "copy", vbf_sb[:], ppv[:, 0:TSL],
                                 waits=[mmv, vbf_reader], inc="ac")
                        vb_ok = em.last("ac")
                        pool.add_reader(slotv, vb_ok)
                        after_use(iv, vb_ok)
                        for j in range(8):
                            mk = em.instr("tensor", "transpose",
                                          ps_tr[:, j * 128 : (j + 1) * 128],
                                          vbf_sb[:, j * 128 : (j + 1) * 128],
                                          ident_sb[:],
                                          waits=[vb_ok, trev_mk[0]] if j == 0 else (),
                                          inc="pe" if j == 7 else None)
                        em.instr("scalar", "copy",
                                 vtm_sb[:, (kvh * NDC + sl * 8) * 128 :
                                        (kvh * NDC + sl * 8 + 8) * 128],
                                 ps_tr[:, 0:1024], waits=[mk], inc="ac")
                        trev_mk[0] = em.last("ac")
                        vbf_reader = trev_mk[0]

                    sc_pend = {}
                    st_ctx = {}

                    def emit_scores(h, qtl):
                        qs = (h // 2 % 2) * 2 + (h % 2)
                        qtg = sl * 8 + qtl
                        lo = max(0, 128 * qtg - 512)
                        wk_ = 128 * qtg + 128 - lo
                        kvh = h // 4
                        pp, wv, slot = pool.alloc(2)
                        n1 = min(wk_, 512)
                        mk = em.instr(
                            "tensor", "matmul", pp[:, 0:n1],
                            qrot_sb[:, qs * TSL + qtl * 128 : qs * TSL + (qtl + 1) * 128],
                            krot_sb[:, kvh * S + lo : kvh * S + lo + n1],
                            start=True, stop=True,
                            waits=[qr_mk[qs]] + wv,
                            inc="pe" if wk_ <= 512 else None)
                        if wk_ > 512:
                            mk = em.instr(
                                "tensor", "matmul", pp[:, 512:wk_],
                                qrot_sb[:, qs * TSL + qtl * 128 : qs * TSL + (qtl + 1) * 128],
                                krot_sb[:, kvh * S + lo + 512 : kvh * S + lo + wk_],
                                start=True, stop=True, inc="pe")
                        sc_pend[(h, qtl)] = (pp, slot, mk, lo, wk_)
                        last_sc[qs] = mk

                    def st1_dve(h, qtl):
                        s_ = h % 2
                        pp, slot, mk, lo, wk_ = sc_pend.pop((h, qtl))
                        qtg = sl * 8 + qtl
                        if qtg >= 4:
                            mask_ap = maskb_sb[:, 0:wk_]
                        else:
                            moff = [0, 128, 384, 768][qtg]
                            mask_ap = maske_sb[:, moff : moff + wk_]
                        scp = sc_sb[:, s_ * 640 : s_ * 640 + wk_]
                        em.instr("vector", "tensor_tensor", scp, pp[:, 0:wk_],
                                 mask_ap, OP.add, waits=[mk, exp_mk[s_]], inc="dv")
                        pool.add_reader(slot, em.last("dv"))
                        em.instr("vector", "reduce_max", nmax_sb[:, s_ : s_ + 1],
                                 scp, axis=AX.X, negate=True, inc="dv")
                        em.instr("vector", "tensor_scalar_mul", nmax_sb[:, s_ : s_ + 1],
                                 nmax_sb[:, s_ : s_ + 1], SCALE_A, inc="dv")
                        st_ctx[(h, qtl)] = {"lo": lo, "wk": wk_, "nm": em.last("dv")}

                    def st1_exp(h, qtl):
                        s_ = h % 2
                        c = st_ctx[(h, qtl)]
                        em.instr("scalar", "activation",
                                 attn_sb[:, s_ * 640 : s_ * 640 + c["wk"]],
                                 sc_sb[:, s_ * 640 : s_ * 640 + c["wk"]], AF.Exp,
                                 scale=SCALE_A, bias=nmax_sb[:, s_ : s_ + 1],
                                 accum_out=rs_sb[:, s_ : s_ + 1],
                                 waits=[c["nm"], rs_war[s_]], inc="ac")
                        exp_mk[s_] = c["exp"] = em.last("ac")

                    def st2_recip(h, qtl):
                        s_ = h % 2
                        c = st_ctx[(h, qtl)]
                        em.instr("vector", "reciprocal", rden_sb[:, s_ : s_ + 1],
                                 rs_sb[:, s_ : s_ + 1], waits=[c["exp"]], inc="dv")
                        c["recip"] = rs_war[s_] = em.last("dv")

                    def st2_norm(h, qtl):
                        s_ = h % 2
                        c = st_ctx[(h, qtl)]
                        em.instr("scalar", "activation",
                                 attn2_sb[:, s_ * 640 : s_ * 640 + c["wk"]],
                                 attn_sb[:, s_ * 640 : s_ * 640 + c["wk"]], AF.Copy,
                                 scale=rden_sb[:, s_ : s_ + 1],
                                 waits=[c["recip"], tr_mk[s_]], inc="ac")
                        c["norm"] = em.last("ac")

                    def st2_tr(h, qtl):
                        s_ = h % 2
                        c = st_ctx[(h, qtl)]
                        nk = c["wk"] // 128
                        for j in range(nk):
                            mk2 = em.instr("tensor", "transpose",
                                           ps_tr[:, s_ * 1024 + j * 128 :
                                                 s_ * 1024 + (j + 1) * 128],
                                           attn2_sb[:, s_ * 640 + j * 128 :
                                                    s_ * 640 + (j + 1) * 128],
                                           ident_sb[:],
                                           waits=[c["norm"], trev_mk[s_]] if j == 0 else (),
                                           inc="pe" if j == nk - 1 else None)
                        tr_mk[s_] = c["tr"] = mk2

                    def st2_copy(h, qtl):
                        s_ = h % 2
                        c = st_ctx[(h, qtl)]
                        em.instr("vector", "tensor_copy",
                                 attnT_sb[:, s_ * 640 : s_ * 640 + c["wk"]],
                                 ps_tr[:, s_ * 1024 : s_ * 1024 + c["wk"]],
                                 waits=[c["tr"], av_mk[s_]], inc="dv")
                        trev_mk[s_] = c["copy"] = em.last("dv")

                    def st2_av(h, qtl):
                        s_ = h % 2
                        c = st_ctx[(h, qtl)]
                        nk = c["wk"] // 128
                        ppav, wvav, slotav = pool.alloc(1)
                        kvh = h // 4
                        for j in range(nk):
                            kt = c["lo"] // 128 + j
                            mk3 = em.instr(
                                "tensor", "matmul", ppav[:, 0:128],
                                vtm_sb[:, (kvh * NDC + kt) * 128 :
                                       (kvh * NDC + kt + 1) * 128],
                                attnT_sb[:, s_ * 640 + j * 128 :
                                         s_ * 640 + (j + 1) * 128],
                                start=(j == 0), stop=(j == nk - 1),
                                waits=([c["copy"]] + wvav) if j == 0 else (),
                                inc="pe" if j == nk - 1 else None)
                        av_mk[s_] = c["av"] = mk3
                        c["avslot"] = slotav

                    def st2_evac(h, qtl):
                        s_ = h % 2
                        c = st_ctx.pop((h, qtl))
                        em.instr("vector", "tensor_copy",
                                 attnout_sb[:, h * TSL + qtl * 128 :
                                            h * TSL + (qtl + 1) * 128],
                                 ppav_of(c), waits=[c["av"]], inc="dv")
                        pool.add_reader(c["avslot"], em.last("dv"))
                        ao_last[0] = em.last("dv")

                    def ppav_of(c):
                        start, nb = c["avslot"]
                        return psum[:, start * 512 : start * 512 + 128]

                    exp_mk = [None, None]
                    tr_mk = [None, None]
                    av_mk = [None, None]
                    rs_war = [None, None]

                    def proj_rope_q(pair_, s_):
                        h = 2 * pair_ + s_
                        qs = (pair_ % 2) * 2 + s_
                        ppq, slotq, mmq, iq = proj_1024(8 + h, [])
                        qr_mk[qs] = rope(ppq,
                                         qrot_sb[:, qs * TSL : (qs + 1) * TSL],
                                         mmq, war=[last_sc[qs]])
                        pool.add_reader(slotq, qr_mk[qs])
                        after_use(iq, qr_mk[qs])

                    for pair in range(NH // 2):
                        if pair == 0:
                            proj_rope_q(0, 0)
                            proj_rope_q(0, 1)
                        hA, hB = 2 * pair, 2 * pair + 1
                        emit_scores(hA, 0)
                        emit_scores(hB, 0)
                        st1_dve(hA, 0)
                        st1_dve(hB, 0)
                        for qtl in range(8):
                            if qtl == 3 and pair + 1 < NH // 2:
                                proj_rope_q(pair + 1, 0)
                            if qtl == 5 and pair + 1 < NH // 2:
                                proj_rope_q(pair + 1, 1)
                            if qtl >= 1:
                                st2_recip(hA, qtl - 1)
                                st2_recip(hB, qtl - 1)
                                st2_norm(hA, qtl - 1)
                                st2_norm(hB, qtl - 1)
                                st2_tr(hA, qtl - 1)
                                st2_tr(hB, qtl - 1)
                            st1_exp(hA, qtl)
                            st1_exp(hB, qtl)
                            if qtl + 1 < 8:
                                emit_scores(hA, qtl + 1)
                                emit_scores(hB, qtl + 1)
                                st1_dve(hA, qtl + 1)
                                st1_dve(hB, qtl + 1)
                            if qtl >= 1:
                                st2_copy(hA, qtl - 1)
                                st2_copy(hB, qtl - 1)
                                st2_av(hA, qtl - 1)
                                st2_av(hB, qtl - 1)
                                st2_evac(hA, qtl - 1)
                                st2_evac(hB, qtl - 1)
                        for hh in (hA, hB):
                            st2_recip(hh, 7)
                            st2_norm(hh, 7)
                            st2_tr(hh, 7)
                            st2_copy(hh, 7)
                            st2_av(hh, 7)
                            st2_evac(hh, 7)
                    ao_ok = ao_last[0]

                    for eo in range(NDC):
                        i, loaded = consume(24 + eo)
                        pp, wv, slot = pool.alloc(2)
                        for nh in range(2):
                            for ei in range(NH):
                                mk = em.instr(
                                    "tensor", "matmul",
                                    pp[:, nh * 512 : nh * 512 + 512],
                                    ws.slot_ap(i)[:, ei * 128 : (ei + 1) * 128],
                                    attnout_sb[:, ei * TSL + nh * 512 :
                                               ei * TSL + nh * 512 + 512],
                                    start=(ei == 0), stop=(ei == NH - 1),
                                    waits=([loaded, ao_ok] + wv)
                                    if (nh == 0 and ei == 0) else (),
                                    inc="pe" if (nh == 1 and ei == NH - 1) else None,
                                )
                        after_use(i, mk)
                        em.instr("vector", "tensor_add", xc(eo, sl), xc(eo, sl),
                                 pp[:, 0:TSL], waits=[mk], inc="dv")
                        pool.add_reader(slot, em.last("dv"))
                    res_mark = em.last("dv")
                    rope_done = res_mark
        return res_mark

    def emit_flush(x_ready):
        with ExitStack() as ph:
            P = ph.enter_context
            stage = P(nc.sbuf_tensor("fl_stage", [128, 2 * TSL], F32))
            dma_mk = [None] * 2
            k = 0
            for dc in range(NDC):
                for sl in range(NSL):
                    sslot = stage[:, (k % 2) * TSL : (k % 2 + 1) * TSL]
                    em.instr("vector", "tensor_copy", sslot, xc(dc, sl),
                             waits=[x_ready, dma_mk[k % 2]], inc="dv")
                    osem = "oA" if k % 2 == 0 else "oB"
                    em.instr("sync", "dma_start",
                             y[dc, :, sl * TSL : (sl + 1) * TSL], sslot,
                             waits=[em.last("dv")], inc=osem, incn=16)
                    dma_mk[k % 2] = em.last(osem)
                    k += 1
            em.wait("sync", em.last("oA"), em.last("oB"))

    def phase_gate():
        snap = [em.last(s) for s in ("pe", "dv", "ac")]
        for e in ("gpsimd", "sync", "tensor", "vector", "scalar"):
            em.wait(e, *snap)

    mark = emit_attn_a(x_loaded)
    if n_phases >= 2:
        phase_gate()
        mark = emit_ffn(mark, final=False, tag="f1")
    if n_phases >= 3:
        phase_gate()
        mark = emit_attn_b(mark)
    if n_phases >= 4:
        phase_gate()
        emit_ffn(mark, final=True, tag="f2")
    else:
        phase_gate()
        emit_flush(mark)

    with nc.Block() as block:
        @block.sync
        def _(e):
            for fn in em.ops["sync"]:
                fn(e)

        @block.gpsimd
        def _(e):
            for fn in em.ops["gpsimd"]:
                fn(e)

        @block.tensor
        def _(e):
            for fn in em.ops["tensor"]:
                fn(e)

        @block.vector
        def _(e):
            for fn in em.ops["vector"]:
                fn(e)

        @block.scalar
        def _(e):
            for fn in em.ops["scalar"]:
                fn(e)

    st.close()
    return nc


def _prep_shared(ins):
    f32 = np.float32
    out = {}
    t44 = lambda a, r, c: np.ascontiguousarray(
        a.reshape(r, 128, c, 128).transpose(2, 0, 1, 3))
    out["wqa"] = t44(np.ascontiguousarray(ins["L0_attn_a_Wq"].T), NDC, NDC)
    out["wka"] = t44(np.ascontiguousarray(ins["L0_attn_a_Wk"].T), NDC, NDC)
    out["wva"] = t44(np.ascontiguousarray(ins["L0_attn_a_Wv"].T), NDC, NDC)
    out["woa"] = t44(np.ascontiguousarray(ins["L0_attn_a_Wo"].T), NDC, NDC)
    cols = lambda v, n: np.ascontiguousarray(v.reshape(n, 128).T, dtype=f32)
    out["biases"] = np.concatenate(
        [cols(ins["L0_attn_a_b" + k], 16) for k in "qkvo"], axis=1)
    out["wg"] = t44(np.ascontiguousarray(ins["L0_ffn_wq_gate"].T), NDC, NHC)
    out["wu"] = t44(np.ascontiguousarray(ins["L0_ffn_wq_up"].T), NDC, NHC)
    out["wd"] = t44(np.ascontiguousarray(ins["L0_ffn_wq_down"].T), NHC, NDC)
    out["sgu"] = np.concatenate(
        [cols(ins["L0_ffn_s_gate"], NHC), cols(ins["L0_ffn_s_up"], NHC)], axis=1)
    out["sdn"] = cols(ins["L0_ffn_s_down"], NDC)
    perm = np.concatenate([np.arange(0, 128, 2), np.arange(1, 128, 2)])
    qp = ins["L0_attn_b_wq"].reshape(D, NH, 128)[:, :, perm]
    kp = ins["L0_attn_b_wk"].reshape(D, NKV, 128)[:, :, perm]
    vp = ins["L0_attn_b_wv"].reshape(D, NKV, 128)
    out["wqb"] = np.ascontiguousarray(
        qp.transpose(1, 0, 2).reshape(NH, NDC, 128, 128))
    out["wkb"] = np.ascontiguousarray(
        kp.transpose(1, 0, 2).reshape(NKV, NDC, 128, 128))
    out["wvb"] = np.ascontiguousarray(
        vp.transpose(1, 0, 2).reshape(NKV, NDC, 128, 128))
    out["wob"] = t44(ins["L0_attn_b_wo"], NDC, NH)
    out["lnw"] = cols(ins["L0_attn_b_ln_w"], NDC)
    out["ones_c"] = np.ones((128, 128), dtype=ml_dtypes.bfloat16)
    out["ident_c"] = np.eye(128, dtype=ml_dtypes.bfloat16)
    out["neg128_c"] = np.full((1, 128), -128.0, dtype=ml_dtypes.bfloat16)
    qi = np.arange(128)[:, None]
    maske = np.zeros((128, 1280), dtype=f32)
    off = 0
    for qt in range(4):
        w = 128 * (qt + 1)
        kj = np.arange(w)[None, :]
        maske[:, off : off + w] = np.where(kj <= qi + 128 * qt, 0.0, -1e30)
        off += w
    out["maske_c"] = maske.astype(ml_dtypes.bfloat16)
    kj = np.arange(640)[None, :]
    out["maskb_c"] = np.where((kj >= qi) & (kj <= qi + WIN), 0.0, -1e30).astype(ml_dtypes.bfloat16)
    invf = 1.0 / 10000.0 ** (np.arange(0, HD, 2, dtype=np.float64) / HD)
    ang = np.arange(S, dtype=np.float64)[:, None] * invf[None, :]
    cosT = np.cos(ang).T.astype(ml_dtypes.bfloat16)
    sinT = np.sin(ang).T.astype(ml_dtypes.bfloat16)
    out["cos2_c"] = np.ascontiguousarray(np.concatenate([cosT, cosT], axis=0))
    out["sin2_c"] = np.ascontiguousarray(np.concatenate([sinT, -sinT], axis=0))
    return out


def run(inputs, n_phases=4, trace=False):
    shared = _prep_shared(inputs)
    x = np.asarray(inputs["L0_attn_a_x"], dtype=np.float32)
    B = x.shape[0]
    in_maps = []
    for b in range(B):
        m = dict(shared)
        m["xT"] = np.ascontiguousarray(x[b].T).reshape(NDC, 128, S)
        in_maps.append(m)
    nc = build_program(n_phases=n_phases)
    res = run_bass_kernel_spmd(nc, in_maps, core_ids=list(range(8)), trace=trace)
    outs = []
    for b in range(B):
        yb = res.results[b]["y"]
        outs.append(yb.reshape(D, S).T)
    return np.stack(outs, axis=0), res


def kernel(**inputs):
    out, _ = run(inputs, n_phases=4)
    return out.astype(np.float32)


# version 10
# speedup vs baseline: 1.1471x; 1.1471x over previous
import sys
from contextlib import ExitStack

import numpy as np
import ml_dtypes

sys.path.insert(0, "/opt/trn_rl_repo")
import concourse.bass as bass
import concourse.mybir as mybir
from concourse.bass_utils import run_bass_kernel_spmd
from concourse.alu_op_type import AluOpType as OP

F32 = mybir.dt.float32
BF16 = mybir.dt.bfloat16
I32 = mybir.dt.int32
AF = mybir.ActivationFunctionType
AX = mybir.AxisListType

D = 2048
S = 2048
NDC = D // 128
TSL = 1024
NSL = S // TSL
HID = 5632
NHC = HID // 128
NH = 16
NKV = 4
HD = 128
WIN = 512
SCALE_A = 128.0 ** -0.5
LN_SCALE_A = float(np.log(SCALE_A))

ENGINES = ("sync", "gpsimd", "tensor", "vector", "scalar")

DBG = {"enable": False}


class Em:

    def __init__(self, nc):
        self.nc = nc
        self.ops = {e: [] for e in ENGINES}
        self.sems = {}
        self.cnt = {}
        self.waited = {e: {} for e in ENGINES}

    def add_sem(self, name, handle):
        self.sems[name] = handle
        self.cnt[name] = 0

    def instr(self, eng, method, *args, waits=(), inc=None, incn=1, **kw):
        ws = []
        for wv in waits:
            if wv is None:
                continue
            s, v = wv
            if v is None or v <= 0:
                continue
            if self.waited[eng].get(s, 0) >= v:
                continue
            ws.append((s, v))
            self.waited[eng][s] = v
        marker = None
        if inc is not None:
            self.cnt[inc] += incn
            marker = (inc, self.cnt[inc])
        sems = self.sems

        def fn(e, ws=tuple(ws), method=method, args=args, kw=kw, inc=inc, incn=incn):
            for s, v in ws:
                e.wait_ge(sems[s], v)
            ins = getattr(e, method)(*args, **kw)
            if inc is not None:
                ins.then_inc(sems[inc], incn)

        self.ops[eng].append(fn)
        return marker

    def wait(self, eng, *waits):
        ws = []
        for wv in waits:
            if wv is None:
                continue
            s, v = wv
            if v is None or v <= 0:
                continue
            if self.waited[eng].get(s, 0) >= v:
                continue
            ws.append((s, v))
            self.waited[eng][s] = v
        if not ws:
            return
        sems = self.sems

        def fn(e, ws=tuple(ws)):
            for s, v in ws:
                e.wait_ge(sems[s], v)

        self.ops[eng].append(fn)

    def last(self, sem):
        return (sem, self.cnt[sem])


class PsumPool:

    def __init__(self, tensor_handle, nbanks):
        self.t = tensor_handle
        self.nbanks = nbanks
        self.readers = [[] for _ in range(nbanks)]
        self.ptr = 0

    def alloc(self, nb):
        if self.ptr + nb > self.nbanks:
            self.ptr = 0
        start = self.ptr
        self.ptr += nb
        waits = []
        for b in range(start, start + nb):
            waits.extend(self.readers[b])
            self.readers[b] = []
        ap = self.t[:, start * 512 : (start + nb) * 512]
        return ap, waits, (start, nb)

    def add_reader(self, slot, marker):
        start, nb = slot
        for b in range(start, start + nb):
            self.readers[b].append(marker)


def build_program(n_phases=4):
    nc = bass.Bass("TRN2", target_bir_lowering=False)

    xT = nc.dram_tensor("xT", [NDC, 128, S], F32, kind="ExternalInput").ap()
    wqa = nc.dram_tensor("wqa", [NDC, NDC, 128, 128], F32, kind="ExternalInput").ap()
    wka = nc.dram_tensor("wka", [NDC, NDC, 128, 128], F32, kind="ExternalInput").ap()
    wva = nc.dram_tensor("wva", [NDC, NDC, 128, 128], F32, kind="ExternalInput").ap()
    woa = nc.dram_tensor("woa", [NDC, NDC, 128, 128], F32, kind="ExternalInput").ap()
    biases = nc.dram_tensor("biases", [128, 64], F32, kind="ExternalInput").ap()
    wg = nc.dram_tensor("wg", [NHC, NDC, 128, 128], I32, kind="ExternalInput").ap()
    wu = nc.dram_tensor("wu", [NHC, NDC, 128, 128], I32, kind="ExternalInput").ap()
    wd = nc.dram_tensor("wd", [NDC, NHC, 128, 128], I32, kind="ExternalInput").ap()
    sgu = nc.dram_tensor("sgu", [128, 2 * NHC], F32, kind="ExternalInput").ap()
    sdn = nc.dram_tensor("sdn", [128, NDC], F32, kind="ExternalInput").ap()
    wqb = nc.dram_tensor("wqb", [NH, NDC, 128, 128], F32, kind="ExternalInput").ap()
    wkb = nc.dram_tensor("wkb", [NKV, NDC, 128, 128], F32, kind="ExternalInput").ap()
    wvb = nc.dram_tensor("wvb", [NKV, NDC, 128, 128], F32, kind="ExternalInput").ap()
    wob = nc.dram_tensor("wob", [NDC, NH, 128, 128], F32, kind="ExternalInput").ap()
    lnw = nc.dram_tensor("lnw", [128, NDC], F32, kind="ExternalInput").ap()
    ones_c = nc.dram_tensor("ones_c", [128, 128], BF16, kind="ExternalInput").ap()
    ident_c = nc.dram_tensor("ident_c", [128, 128], BF16, kind="ExternalInput").ap()
    neg128_c = nc.dram_tensor("neg128_c", [1, 128], BF16, kind="ExternalInput").ap()
    maske_c = nc.dram_tensor("maske_c", [128, 1280], BF16, kind="ExternalInput").ap()
    maskb_c = nc.dram_tensor("maskb_c", [128, 640], BF16, kind="ExternalInput").ap()
    cos2_c = nc.dram_tensor("cos2_c", [128, S], BF16, kind="ExternalInput").ap()
    sin2_c = nc.dram_tensor("sin2_c", [128, S], BF16, kind="ExternalInput").ap()
    y = nc.dram_tensor("y", [NDC, 128, S], F32, kind="ExternalOutput").ap()
    if DBG["enable"]:
        ydbg = nc.dram_tensor("ydbg", [3, 8, 128, 640], F32, kind="ExternalOutput").ap()
        aodbg = nc.dram_tensor("aodbg", [128, NH * TSL], F32, kind="ExternalOutput").ap()

    st = ExitStack()
    E = st.enter_context
    em = Em(nc)
    for sname in ("dX", "dC", "oA", "oB", "pe", "dv", "ac", "gp",
                  "w0", "w1", "w2", "w3", "w4", "w5", "dD"):
        em.add_sem(sname, E(nc.semaphore(sname)))

    x_sb = E(nc.sbuf_tensor("x_sb", [128, NDC * S], BF16))
    ones_sb = E(nc.sbuf_tensor("ones_sb", [128, 128], BF16))
    ident_sb = E(nc.sbuf_tensor("ident_sb", [128, 128], BF16))
    neg128_sb = E(nc.sbuf_tensor("neg128_sb", [1, 128], BF16))
    b_sb = E(nc.sbuf_tensor("b_sb", [128, 64], F32))
    bqs_sb = E(nc.sbuf_tensor("bqs_sb", [128, 16], F32))
    sgu_sb = E(nc.sbuf_tensor("sgu_sb", [128, 2 * NHC], F32))
    sdn_sb = E(nc.sbuf_tensor("sdn_sb", [128, NDC], F32))
    lnw_sb = E(nc.sbuf_tensor("lnw_sb", [128, NDC], F32))

    def xc(dc, sl=None):
        if sl is None:
            return x_sb[:, dc * S : (dc + 1) * S]
        return x_sb[:, dc * S + sl * TSL : dc * S + sl * TSL + TSL]

    for dc in range(NDC):
        em.instr("gpsimd", "dma_start", x_sb[:, dc * S : (dc + 1) * S], xT[dc, :, :],
                 inc="dX", incn=16)
    x_loaded = em.last("dX")
    for dst, src in ((ones_sb, ones_c), (ident_sb, ident_c), (neg128_sb, neg128_c),
                     (b_sb, biases), (sgu_sb, sgu), (sdn_sb, sdn), (lnw_sb, lnw)):
        em.instr("sync", "dma_start", dst[:], src, inc="dC", incn=16)
    consts_loaded = em.last("dC")
    em.instr("vector", "tensor_scalar_mul", bqs_sb[:, :], b_sb[:, 0:16], SCALE_A,
             waits=[consts_loaded], inc="dv")
    constv_sb = E(nc.sbuf_tensor("constv_sb", [128, 4], F32))
    for j, v in enumerate((1e-5, 1e-11, LN_SCALE_A, -25.0)):
        em.instr("vector", "memset", constv_sb[:, j : j + 1], v, inc="dv")
        nc.const_aps.register(F32, v, constv_sb[:, j : j + 1])
    consts_ready = em.last("dv")
    em.wait("tensor", consts_loaded)
    em.wait("scalar", consts_ready)

    def emit_ln(sl, xn_sb, pool, kind, stat, xsq_sb, x_ready):
        ps_sum, w1, slot1 = pool.alloc(2)
        ps_sq, w2, slot2 = pool.alloc(2)
        sq_mk = [None] * NDC
        for dc in range(NDC):
            pp = xsq_sb[:, (dc % 2) * TSL : (dc % 2 + 1) * TSL]
            war = sq_mk[dc - 2] if dc >= 2 else None
            em.instr("vector", "tensor_mul", pp, xc(dc, sl), xc(dc, sl),
                     waits=[x_ready, war], inc="dv")
            sq_ready = em.last("dv")
            for nh in range(2):
                cols = slice(nh * 512, nh * 512 + 512)
                em.instr("tensor", "matmul", ps_sum[:, cols], ones_sb[:, :],
                         xc(dc, sl)[:, cols], start=(dc == 0), stop=(dc == NDC - 1),
                         waits=[x_ready] + (w1 if dc == 0 else []))
                sq_mk[dc] = em.instr("tensor", "matmul", ps_sq[:, cols], ones_sb[:, :],
                                     pp[:, cols], start=(dc == 0), stop=(dc == NDC - 1),
                                     waits=[sq_ready] + (w2 if dc == 0 else []),
                                     inc="pe")
        sums_done = em.last("pe")
        m_b, c_b, t_b, u_b = stat["m"], stat["c"], stat["t"], stat["u"]
        em.instr("vector", "tensor_scalar_mul", m_b[:], ps_sum[:, 0:TSL], 1.0 / D,
                 waits=[sums_done], inc="dv")
        em.instr("vector", "tensor_scalar_mul", t_b[:], ps_sq[:, 0:TSL], 1.0 / D, inc="dv")
        em.instr("vector", "tensor_mul", u_b[:], m_b[:], m_b[:], inc="dv")
        em.instr("vector", "tensor_sub", t_b[:], t_b[:], u_b[:], inc="dv")
        var_done = em.last("dv")
        pool.add_reader(slot1, var_done)
        pool.add_reader(slot2, var_done)
        if kind == "attnb":
            a_, b_ = 1.000001, 1e-11
        else:
            a_, b_ = 1.0, 1e-5
        em.instr("scalar", "activation", t_b[:], t_b[:], AF.Ln, scale=a_, bias=b_,
                 waits=[var_done], inc="ac")
        em.instr("scalar", "activation", c_b[:], t_b[:], AF.Exp, scale=-0.5, inc="ac")
        c_done = em.last("ac")
        for dc in range(NDC):
            xn = xn_sb[:, dc * TSL : (dc + 1) * TSL]
            em.instr("vector", "tensor_sub", u_b[:], xc(dc, sl), m_b[:],
                     waits=[c_done], inc="dv")
            if kind == "attnb":
                em.instr("vector", "tensor_mul", u_b[:], u_b[:], c_b[:], inc="dv")
                em.instr("vector", "tensor_scalar_mul", xn, u_b[:],
                         lnw_sb[:, dc : dc + 1], inc="dv")
            else:
                em.instr("vector", "tensor_mul", xn, u_b[:], c_b[:], inc="dv")
        return em.last("dv")

    class WStream:

        def __init__(self, buf, nslots, blkcols, sems):
            self.buf = buf
            self.nslots = nslots
            self.blkcols = blkcols
            self.readers = [None] * nslots
            self.sems = sems
            self.i = 0

        def slot_ap(self, i):
            c = (i % self.nslots) * self.blkcols
            return self.buf[:, c : c + self.blkcols]

        def load(self, src_tiled, blk, nchunks):
            return self.load_ap(src_tiled[blk], nchunks)

        def load_ap(self, ap3d, nchunks):
            i = self.i
            self.i += 1
            sem = self.sems[i % self.nslots]
            em.instr("gpsimd", "dma_start",
                     self.slot_ap(i).rearrange("p (c q) -> p c q", c=nchunks),
                     ap3d.rearrange("c p q -> p c q"),
                     waits=[self.readers[i % self.nslots]], inc=sem, incn=16)
            return i, em.last(sem)

        def set_reader(self, i, marker):
            self.readers[i % self.nslots] = marker

    def emit_attn_a(x_ready):
        with ExitStack() as ph:
            P = ph.enter_context
            xn_sb = P(nc.sbuf_tensor("a_xn", [128, NDC * TSL], BF16))
            ctx_sb = P(nc.sbuf_tensor("a_ctx", [128, NDC * TSL], BF16))
            phiq_sb = P(nc.sbuf_tensor("a_phiq", [128, TSL], BF16))
            scrA = P(nc.sbuf_tensor("a_scrA", [128, TSL], F32))
            scrB = P(nc.sbuf_tensor("a_scrB", [128, TSL], F32))
            scrC = P(nc.sbuf_tensor("a_scrC", [128, TSL], F32))
            zeros_sb = P(nc.sbuf_tensor("a_zeros", [128, TSL], F32))
            carry_sb = P(nc.sbuf_tensor("a_carry", [128, NDC], F32))
            xsq_sb = P(nc.sbuf_tensor("a_xsq", [128, 2 * TSL], BF16))
            stat = {k: P(nc.sbuf_tensor("a_st_" + k, [128, TSL],
                                        F32 if k == "t" else BF16))
                    for k in ("m", "c", "t", "u")}
            wbuf = P(nc.sbuf_tensor("a_wbuf", [128, 6 * NDC * 128], BF16))
            psum = P(nc.psum_tensor("a_psum", [128, 8 * 512], F32))
            pool = PsumPool(psum, 8)
            ws = WStream(wbuf, 6, NDC * 128, ["w0", "w1", "w2", "w3", "w4", "w5"])

            em.instr("vector", "memset", zeros_sb[:], 0.0, waits=[consts_ready], inc="dv")
            em.instr("gpsimd", "memset", carry_sb[:], 0.0, inc="gp")
            carry_mark = [em.last("gp")] * NDC

            pending = {}

            def prefetch(sl, ec):
                for name, src in (("q", wqa), ("k", wka), ("v", wva)):
                    pending[(sl, ec, name)] = ws.load(src, ec, NDC)

            prefetch(0, 0)
            res_mark = x_ready
            for sl in range(NSL):
                xn_ready = emit_ln(sl, xn_sb, pool, "plain", stat, xsq_sb, res_mark)
                last_ctx = None
                for ec in range(NDC):
                    if ec + 1 < NDC:
                        prefetch(sl, ec + 1)
                    ps = {}
                    for name in ("q", "k", "v"):
                        i, loaded = pending.pop((sl, ec, name))
                        pp, wv, slot = pool.alloc(2)
                        for nh in range(2):
                            for dc in range(NDC):
                                mk = em.instr(
                                    "tensor", "matmul",
                                    pp[:, nh * 512 : nh * 512 + 512],
                                    ws.slot_ap(i)[:, dc * 128 : (dc + 1) * 128],
                                    xn_sb[:, dc * TSL + nh * 512 : dc * TSL + nh * 512 + 512],
                                    start=(dc == 0), stop=(dc == NDC - 1),
                                    waits=([loaded, xn_ready] + wv) if (nh == 0 and dc == 0) else (),
                                    inc="pe" if (nh == 1 and dc == NDC - 1) else None,
                                )
                        ws.set_reader(i, mk)
                        ps[name] = (pp, slot, mk)
                    ppq, slotq, mmq = ps["q"]
                    ppk, slotk, mmk = ps["k"]
                    ppv, slotv, mmv = ps["v"]
                    bq_ap = b_sb[:, ec : ec + 1]
                    bqs_ap = bqs_sb[:, ec : ec + 1]
                    bk_ap = b_sb[:, 16 + ec : 16 + ec + 1]
                    bv_ap = b_sb[:, 32 + ec : 32 + ec + 1]
                    em.instr("scalar", "activation", scrA[:], ppq[:, 0:TSL], AF.Relu,
                             scale=SCALE_A, bias=bqs_ap, waits=[mmq, em.last("dv")],
                             inc="ac")
                    rq_done = em.last("ac")
                    em.instr("vector", "tensor_scalar", scrB[:], ppq[:, 0:TSL],
                             bq_ap, 0.0, OP.add, OP.min, waits=[mmq, rq_done], inc="dv")
                    mnq_done = em.last("dv")
                    pool.add_reader(slotq, mnq_done)
                    em.instr("scalar", "activation", scrC[:], scrB[:], AF.Exp,
                             bias=LN_SCALE_A, waits=[mnq_done], inc="ac")
                    eq_done = em.last("ac")
                    em.instr("vector", "tensor_add", phiq_sb[:], scrA[:], scrC[:],
                             waits=[eq_done], inc="dv")
                    em.instr("scalar", "activation", scrA[:], ppk[:, 0:TSL], AF.Relu,
                             bias=bk_ap, waits=[mmk, em.last("dv")], inc="ac")
                    rk_done = em.last("ac")
                    em.instr("vector", "tensor_scalar", scrB[:], ppk[:, 0:TSL],
                             bk_ap, 0.0, OP.add, OP.min, waits=[mmk, rk_done], inc="dv")
                    mnk_done = em.last("dv")
                    pool.add_reader(slotk, mnk_done)
                    em.instr("scalar", "activation", scrC[:], scrB[:], AF.Exp,
                             waits=[mnk_done], inc="ac")
                    ek_done = em.last("ac")
                    em.instr("vector", "tensor_add", scrA[:], scrA[:], scrC[:],
                             waits=[ek_done], inc="dv")
                    em.instr("vector", "scalar_tensor_tensor", scrB[:], ppv[:, 0:TSL],
                             bv_ap, scrA[:], OP.add, OP.mult, waits=[mmv], inc="dv")
                    pool.add_reader(slotv, em.last("dv"))
                    em.instr("vector", "tensor_tensor_scan", scrC[:], scrB[:], zeros_sb[:],
                             carry_sb[:, ec : ec + 1], OP.add, OP.add,
                             waits=[carry_mark[ec]], inc="dv")
                    scan_done = em.last("dv")
                    if sl + 1 < NSL:
                        em.instr("scalar", "copy", carry_sb[:, ec : ec + 1],
                                 scrC[:, TSL - 1 : TSL], waits=[scan_done], inc="ac")
                        carry_mark[ec] = em.last("ac")
                    em.instr("vector", "tensor_mul",
                             ctx_sb[:, ec * TSL : (ec + 1) * TSL], phiq_sb[:], scrC[:],
                             inc="dv")
                    last_ctx = em.last("dv")
                wo_pending = {}
                wo_pending[0] = ws.load(woa, 0, NDC)
                wo_pending[1] = ws.load(woa, 1, NDC)
                for eo in range(NDC):
                    if eo + 2 < NDC:
                        wo_pending[eo + 2] = ws.load(woa, eo + 2, NDC)
                    elif eo == NDC - 2 and sl + 1 < NSL:
                        prefetch(sl + 1, 0)
                    i, loaded = wo_pending.pop(eo)
                    pp, wv, slot = pool.alloc(2)
                    for nh in range(2):
                        for ec in range(NDC):
                            mk = em.instr(
                                "tensor", "matmul",
                                pp[:, nh * 512 : nh * 512 + 512],
                                ws.slot_ap(i)[:, ec * 128 : (ec + 1) * 128],
                                ctx_sb[:, ec * TSL + nh * 512 : ec * TSL + nh * 512 + 512],
                                start=(ec == 0), stop=(ec == NDC - 1),
                                waits=([loaded, last_ctx] + wv) if (nh == 0 and ec == 0) else (),
                                inc="pe" if (nh == 1 and ec == NDC - 1) else None,
                            )
                    ws.set_reader(i, mk)
                    bo_ap = b_sb[:, 48 + eo : 48 + eo + 1]
                    em.instr("vector", "scalar_tensor_tensor", xc(eo, sl),
                             pp[:, 0:TSL], bo_ap, xc(eo, sl), OP.add, OP.add,
                             waits=[mk], inc="dv")
                    pool.add_reader(slot, em.last("dv"))
                res_mark = em.last("dv")
        return res_mark

    def emit_ffn(x_ready, final, tag):
        with ExitStack() as ph:
            P = ph.enter_context
            psum = P(nc.psum_tensor(tag + "_psum", [128, 8 * 512], F32))
            pool = PsumPool(psum, 6)
            cs_ps = psum[:, 6 * 512 : 8 * 512]
            res_mark = x_ready
            gu_done = None
            cs_reader = None
            stage_mk = [None, None]
            for sl in range(NSL):
                with ExitStack() as slsc:
                    Q = slsc.enter_context
                    xnsc = ExitStack()
                    X_ = xnsc.enter_context
                    xn_sb = X_(nc.sbuf_tensor(f"{tag}{sl}_xn", [128, NDC * TSL], BF16))
                    with ExitStack() as lnsc:
                        L = lnsc.enter_context
                        stat = {k: L(nc.sbuf_tensor(f"{tag}{sl}_st_{k}", [128, TSL],
                                            F32 if k == "t" else BF16))
                                for k in ("m", "c", "t", "u")}
                        xsq_sb = L(nc.sbuf_tensor(f"{tag}{sl}_xsq", [128, 2 * TSL], BF16))
                        xn_ready = emit_ln(sl, xn_sb, pool, "plain", stat, xsq_sb,
                                           res_mark)
                    act_sb = Q(nc.sbuf_tensor(f"{tag}{sl}_act", [128, NHC * TSL], BF16,
                                              side="right"))
                    with ExitStack() as gusc:
                        G = gusc.enter_context
                        gu_buf = G(nc.sbuf_tensor(f"{tag}{sl}_gu", [128, 6 * 8 * 128], BF16))
                        g_scr = G(nc.sbuf_tensor(f"{tag}{sl}_g", [128, 2 * TSL], BF16))
                        gws = WStream(gu_buf, 6, 8 * 128,
                                      ["w0", "w1", "w2", "w3", "w4", "w5"])
                        gws.readers = [xn_ready] * 6
                        gu_jobs = []
                        for hc_ in range(NHC):
                            for src_ in (wg, wu):
                                for hf in range(2):
                                    gu_jobs.append(src_[hc_, hf * 8 : hf * 8 + 8])
                        gu_pend = {}
                        gu_next = [0]

                        def gu_load():
                            j = gu_next[0]
                            gu_next[0] += 1
                            gu_pend[j] = gws.load_ap(gu_jobs[j], 8)

                        for _ in range(6):
                            gu_load()
                        act_mk = {}
                        g_war = [None, None]

                        def emit_cs_act(hcl):
                            for nh in range(2):
                                em.instr("tensor", "matmul",
                                         cs_ps[:, nh * 512 : nh * 512 + 512],
                                         ones_sb[:, :],
                                         act_sb[:, hcl * TSL + nh * 512 :
                                                hcl * TSL + nh * 512 + 512],
                                         start=(hcl == 0), stop=(hcl == NHC - 1),
                                         waits=[act_mk[hcl], cs_reader]
                                         if hcl == 0 else [act_mk[hcl]],
                                         inc="pe" if (hcl == NHC - 1 and nh == 1) else None)

                        for hc in range(NHC):
                            pps = {}
                            for nmi, name in enumerate(("g", "u")):
                                j0 = (hc * 2 + nmi) * 2
                                i0, ld0 = gu_pend.pop(j0)
                                i1, ld1 = gu_pend.pop(j0 + 1)
                                pp, wv, slot = pool.alloc(2)
                                for nh in range(2):
                                    for dc in range(NDC):
                                        ii = i0 if dc < 8 else i1
                                        mk = em.instr(
                                            "tensor", "matmul",
                                            pp[:, nh * 512 : nh * 512 + 512],
                                            gws.slot_ap(ii)[:, (dc % 8) * 128 : (dc % 8 + 1) * 128],
                                            xn_sb[:, dc * TSL + nh * 512 : dc * TSL + nh * 512 + 512],
                                            start=(dc == 0), stop=(dc == NDC - 1),
                                            waits=([ld0, ld1, xn_ready] + wv)
                                            if (nh == 0 and dc == 0) else (),
                                            inc="pe" if (nh == 1 and dc == NDC - 1) else None,
                                        )
                                ws_mk = mk
                                gws.set_reader(i0, ws_mk)
                                gws.set_reader(i1, ws_mk)
                                pps[name] = (pp, slot, ws_mk)
                                for _ in range(2):
                                    if gu_next[0] < len(gu_jobs):
                                        gu_load()
                            ppg, slotg, mmg = pps["g"]
                            ppu, slotu, mmu = pps["u"]
                            gp = g_scr[:, (hc % 2) * TSL : (hc % 2 + 1) * TSL]
                            em.instr("scalar", "activation", gp, ppg[:, 0:TSL],
                                     AF.Silu, scale=sgu_sb[:, hc : hc + 1],
                                     waits=[mmg, g_war[hc % 2]], inc="ac")
                            g_done = em.last("ac")
                            pool.add_reader(slotg, g_done)
                            em.instr("vector", "scalar_tensor_tensor",
                                     act_sb[:, hc * TSL : (hc + 1) * TSL],
                                     gp, sgu_sb[:, NHC + hc : NHC + hc + 1],
                                     ppu[:, 0:TSL], OP.mult, OP.mult,
                                     waits=[mmu, g_done], inc="dv")
                            act_mk[hc] = g_war[hc % 2] = em.last("dv")
                            pool.add_reader(slotu, act_mk[hc])
                            if hc >= 2:
                                emit_cs_act(hc - 2)
                        for hcl in (NHC - 2, NHC - 1):
                            emit_cs_act(hcl)
                        gu_done = em.last("pe")
                    xnsc.close()
                    with ExitStack() as dnsc:
                        DN = dnsc.enter_context
                        dn_buf = DN(nc.sbuf_tensor(f"{tag}{sl}_dn", [128, 2 * NHC * 128], BF16))
                        ca_row = DN(nc.sbuf_tensor(f"{tag}{sl}_carow", [1, TSL], BF16))
                        if final:
                            stage = DN(nc.sbuf_tensor(f"{tag}{sl}_stage", [128, 2 * TSL], F32))
                        dws = WStream(dn_buf, 2, NHC * 128, ["w2", "w3"])
                        dws.readers = [gu_done, gu_done]
                        em.instr("vector", "tensor_copy", ca_row[0:1, :],
                                 cs_ps[0:1, 0:TSL], waits=[gu_done], inc="dv")
                        ca_done = em.last("dv")
                        cs_reader = ca_done
                        dn_pending = {0: dws.load(wd, 0, NHC)}
                        for dout in range(NDC):
                            if dout + 1 < NDC:
                                dn_pending[dout + 1] = dws.load(wd, dout + 1, NHC)
                            i, loaded = dn_pending.pop(dout)
                            pp, wv, slot = pool.alloc(2)
                            for nh in range(2):
                                for hc in range(NHC):
                                    em.instr(
                                        "tensor", "matmul",
                                        pp[:, nh * 512 : nh * 512 + 512],
                                        dws.slot_ap(i)[:, hc * 128 : (hc + 1) * 128],
                                        act_sb[:, hc * TSL + nh * 512 : hc * TSL + nh * 512 + 512],
                                        start=(hc == 0), stop=False,
                                        waits=([loaded] + wv) if (nh == 0 and hc == 0) else (),
                                    )
                                mk = em.instr(
                                    "tensor", "matmul",
                                    pp[:, nh * 512 : nh * 512 + 512],
                                    neg128_sb[0:1, :],
                                    ca_row[0:1, nh * 512 : nh * 512 + 512],
                                    start=False, stop=True,
                                    waits=[ca_done] if nh == 0 else (),
                                    inc="pe" if nh == 1 else None,
                                )
                            dws.set_reader(i, mk)
                            sd_ap = sdn_sb[:, dout : dout + 1]
                            if final:
                                sslot = stage[:, (dout % 2) * TSL : (dout % 2 + 1) * TSL]
                                em.instr("vector", "scalar_tensor_tensor", sslot,
                                         pp[:, 0:TSL], sd_ap, xc(dout, sl),
                                         OP.mult, OP.add,
                                         waits=[mk, stage_mk[dout % 2]], inc="dv")
                                pool.add_reader(slot, em.last("dv"))
                                osem = "oA" if dout % 2 == 0 else "oB"
                                em.instr("sync", "dma_start",
                                         y[dout, :, sl * TSL : (sl + 1) * TSL], sslot,
                                         waits=[em.last("dv")], inc=osem, incn=16)
                                stage_mk[dout % 2] = em.last(osem)
                            else:
                                em.instr("vector", "scalar_tensor_tensor",
                                         xc(dout, sl), pp[:, 0:TSL], sd_ap,
                                         xc(dout, sl), OP.mult, OP.add,
                                         waits=[mk], inc="dv")
                                pool.add_reader(slot, em.last("dv"))
                        res_mark = em.last("dv")
            if final:
                em.wait("sync", em.last("oA"), em.last("oB"))
            return res_mark

    def emit_attn_b(x_ready):
        with ExitStack() as ph:
            P = ph.enter_context
            maske_sb = P(nc.sbuf_tensor("b_maske", [128, 1280], BF16))
            maskb_sb = P(nc.sbuf_tensor("b_maskb", [128, 640], BF16))
            cos_sb = P(nc.sbuf_tensor("b_cos", [128, TSL], BF16))
            sin_sb = P(nc.sbuf_tensor("b_sin", [128, TSL], BF16))
            krot_sb = P(nc.sbuf_tensor("b_krot", [128, NKV * S], BF16))
            vtm_sb = P(nc.sbuf_tensor("b_vtm", [128, NKV * S], BF16))
            wsbuf = P(nc.sbuf_tensor("b_wsbuf", [128, 2 * NDC * 128], BF16))
            qrot_sb = P(nc.sbuf_tensor("b_qrot", [128, 4 * TSL], BF16))
            vbf_sb = P(nc.sbuf_tensor("b_vbf", [128, TSL], BF16))
            u1_sb = P(nc.sbuf_tensor("b_u1", [128, TSL], BF16))
            u2_sb = P(nc.sbuf_tensor("b_u2", [128, TSL], BF16))
            u2s_sb = P(nc.sbuf_tensor("b_u2s", [128, TSL], BF16))
            sc_sb = P(nc.sbuf_tensor("b_sc", [128, 2 * 640], F32))
            attn_sb = P(nc.sbuf_tensor("b_attn", [128, 2 * 640], BF16))
            attn2_sb = P(nc.sbuf_tensor("b_attn2", [128, 2 * 640], BF16))
            attnT_sb = P(nc.sbuf_tensor("b_attnT", [128, 2 * 640], BF16))
            nmax_sb = P(nc.sbuf_tensor("b_nmax", [128, 2], F32))
            rs_sb = P(nc.sbuf_tensor("b_rs", [128, 2], F32))
            rden_sb = P(nc.sbuf_tensor("b_rden", [128, 2], F32))
            psum = P(nc.psum_tensor("b_psum", [128, 6 * 512], F32))
            ps_tr = P(nc.psum_tensor("b_pstr", [128, 2048], BF16))
            pool = PsumPool(psum, 6)

            em.instr("sync", "dma_start", maske_sb[:], maske_c, waits=[consts_ready],
                     inc="dC", incn=16)
            em.instr("sync", "dma_start", maskb_sb[:], maskb_c, inc="dC", incn=16)
            masks_ok = em.last("dC")
            em.wait("vector", masks_ok)
            em.wait("tensor", masks_ok)

            ws = WStream(wsbuf, 2, NDC * 128, ["w0", "w1"])
            qr_mk = [None] * 4
            last_sc = [None] * 4
            ao_last = [None]
            res_mark = x_ready
            rope_done = None
            trev_mk = [None, None]
            vbf_reader = None

            for sl in range(NSL):
                em.instr("sync", "dma_start", cos_sb[:],
                         cos2_c[:, sl * TSL : (sl + 1) * TSL],
                         waits=[rope_done or consts_ready], inc="dC", incn=16)
                em.instr("sync", "dma_start", sin_sb[:],
                         sin2_c[:, sl * TSL : (sl + 1) * TSL], inc="dC", incn=16)
                cs_ok = em.last("dC")
                jobs = []
                for kvh in range(NKV):
                    jobs.append((wkb, kvh))
                    jobs.append((wvb, kvh))
                for h in range(NH):
                    jobs.append((wqb, h))
                for eo in range(NDC):
                    jobs.append((wob, eo))
                pend = {}
                for j in range(2):
                    pend[j] = ws.load(jobs[j][0], jobs[j][1], NDC)
                nextload = [2]

                def consume(j):
                    i, loaded = pend.pop(j)
                    return i, loaded

                def after_use(i, mk):
                    ws.set_reader(i, mk)
                    if nextload[0] < len(jobs):
                        j = nextload[0]
                        nextload[0] += 1
                        pend[j] = ws.load(jobs[j][0], jobs[j][1], NDC)

                with ExitStack() as slsc:
                    Q = slsc.enter_context
                    xn_sb = Q(nc.sbuf_tensor(f"b{sl}_xn", [128, NDC * TSL], BF16))
                    with ExitStack() as lnsc:
                        L = lnsc.enter_context
                        stat = {k: L(nc.sbuf_tensor(f"b{sl}_st_{k}", [128, TSL],
                                            F32 if k == "t" else BF16))
                                for k in ("m", "c", "t", "u")}
                        xsq_sb = L(nc.sbuf_tensor(f"b{sl}_xsq", [128, 2 * TSL], BF16))
                        xn_ready = emit_ln(sl, xn_sb, pool, "attnb", stat, xsq_sb,
                                           res_mark)
                    attnout_sb = Q(nc.sbuf_tensor(f"b{sl}_ao", [128, NH * TSL], BF16,
                                                  side="right"))

                    def proj_1024(jidx, extra_waits):
                        i, loaded = consume(jidx)
                        pp, wv, slot = pool.alloc(2)
                        for nh in range(2):
                            for dc in range(NDC):
                                mk = em.instr(
                                    "tensor", "matmul",
                                    pp[:, nh * 512 : nh * 512 + 512],
                                    ws.slot_ap(i)[:, dc * 128 : (dc + 1) * 128],
                                    xn_sb[:, dc * TSL + nh * 512 : dc * TSL + nh * 512 + 512],
                                    start=(dc == 0), stop=(dc == NDC - 1),
                                    waits=([loaded, xn_ready] + wv + extra_waits)
                                    if (nh == 0 and dc == 0) else (),
                                    inc="pe" if (nh == 1 and dc == NDC - 1) else None,
                                )
                        return pp, slot, mk, i

                    def rope(pp, dst_ap, mm_mk, war=()):
                        em.instr("vector", "tensor_mul", u1_sb[:], pp[:, 0:TSL],
                                 cos_sb[:], waits=[mm_mk, cs_ok], inc="dv")
                        em.instr("vector", "tensor_mul", u2_sb[:], pp[:, 0:TSL],
                                 sin_sb[:], inc="dv")
                        em.instr("vector", "tensor_copy", u2s_sb[0:64, :],
                                 u2_sb[64:128, :], inc="dv")
                        em.instr("vector", "tensor_copy", u2s_sb[64:128, :],
                                 u2_sb[0:64, :], inc="dv")
                        em.instr("vector", "tensor_add", dst_ap[:, :],
                                 u1_sb[:], u2s_sb[:], waits=list(war), inc="dv")
                        return em.last("dv")

                    for kvh in range(NKV):
                        ppk, slotk, mmk, ik = proj_1024(2 * kvh, [])
                        kdst = krot_sb[:, kvh * S + sl * TSL : kvh * S + (sl + 1) * TSL]
                        kr = rope(ppk, kdst, mmk)
                        pool.add_reader(slotk, kr)
                        after_use(ik, kr)
                        ppv, slotv, mmv, iv = proj_1024(2 * kvh + 1, [])
                        em.instr("scalar", "copy", vbf_sb[:], ppv[:, 0:TSL],
                                 waits=[mmv, vbf_reader], inc="ac")
                        vb_ok = em.last("ac")
                        pool.add_reader(slotv, vb_ok)
                        after_use(iv, vb_ok)
                        for j in range(8):
                            mk = em.instr("tensor", "transpose",
                                          ps_tr[:, j * 128 : (j + 1) * 128],
                                          vbf_sb[:, j * 128 : (j + 1) * 128],
                                          ident_sb[:],
                                          waits=[vb_ok, trev_mk[0]] if j == 0 else (),
                                          inc="pe" if j == 7 else None)
                        em.instr("scalar", "copy",
                                 vtm_sb[:, (kvh * NDC + sl * 8) * 128 :
                                        (kvh * NDC + sl * 8 + 8) * 128],
                                 ps_tr[:, 0:1024], waits=[mk], inc="ac")
                        trev_mk[0] = em.last("ac")
                        vbf_reader = trev_mk[0]

                    sc_pend = {}
                    st_ctx = {}

                    def emit_scores(h, qtl):
                        qs = (h // 2 % 2) * 2 + (h % 2)
                        qtg = sl * 8 + qtl
                        lo = max(0, 128 * qtg - 512)
                        wk_ = 128 * qtg + 128 - lo
                        kvh = h // 4
                        pp, wv, slot = pool.alloc(2)
                        n1 = min(wk_, 512)
                        mk = em.instr(
                            "tensor", "matmul", pp[:, 0:n1],
                            qrot_sb[:, qs * TSL + qtl * 128 : qs * TSL + (qtl + 1) * 128],
                            krot_sb[:, kvh * S + lo : kvh * S + lo + n1],
                            start=True, stop=True,
                            waits=[qr_mk[qs]] + wv,
                            inc="pe" if wk_ <= 512 else None)
                        if wk_ > 512:
                            mk = em.instr(
                                "tensor", "matmul", pp[:, 512:wk_],
                                qrot_sb[:, qs * TSL + qtl * 128 : qs * TSL + (qtl + 1) * 128],
                                krot_sb[:, kvh * S + lo + 512 : kvh * S + lo + wk_],
                                start=True, stop=True, inc="pe")
                        sc_pend[(h, qtl)] = (pp, slot, mk, lo, wk_)
                        last_sc[qs] = mk

                    def st1_dve(h, qtl):
                        s_ = h % 2
                        pp, slot, mk, lo, wk_ = sc_pend.pop((h, qtl))
                        qtg = sl * 8 + qtl
                        if qtg >= 4:
                            mask_ap = maskb_sb[:, 0:wk_]
                        else:
                            moff = [0, 128, 384, 768][qtg]
                            mask_ap = maske_sb[:, moff : moff + wk_]
                        scp = sc_sb[:, s_ * 640 : s_ * 640 + wk_]
                        em.instr("vector", "tensor_tensor", scp, pp[:, 0:wk_],
                                 mask_ap, OP.add, waits=[mk, exp_mk[s_]], inc="dv")
                        pool.add_reader(slot, em.last("dv"))
                        st_ctx[(h, qtl)] = {"lo": lo, "wk": wk_, "nm": em.last("dv")}

                    def st1_exp(h, qtl):
                        s_ = h % 2
                        c = st_ctx[(h, qtl)]
                        em.instr("scalar", "activation",
                                 attn_sb[:, s_ * 640 : s_ * 640 + c["wk"]],
                                 sc_sb[:, s_ * 640 : s_ * 640 + c["wk"]], AF.Exp,
                                 scale=SCALE_A, bias=-25.0,
                                 accum_out=rs_sb[:, s_ : s_ + 1],
                                 waits=[c["nm"], rs_war[s_]], inc="ac")
                        exp_mk[s_] = c["exp"] = em.last("ac")

                    def st2_recip(h, qtl):
                        s_ = h % 2
                        c = st_ctx[(h, qtl)]
                        em.instr("vector", "reciprocal", rden_sb[:, s_ : s_ + 1],
                                 rs_sb[:, s_ : s_ + 1], waits=[c["exp"]], inc="dv")
                        c["recip"] = rs_war[s_] = em.last("dv")

                    def st2_norm(h, qtl):
                        s_ = h % 2
                        c = st_ctx[(h, qtl)]
                        em.instr("scalar", "activation",
                                 attn2_sb[:, s_ * 640 : s_ * 640 + c["wk"]],
                                 attn_sb[:, s_ * 640 : s_ * 640 + c["wk"]], AF.Copy,
                                 scale=rden_sb[:, s_ : s_ + 1],
                                 waits=[c["recip"], tr_mk[s_]], inc="ac")
                        c["norm"] = em.last("ac")

                    def st2_tr(h, qtl):
                        s_ = h % 2
                        c = st_ctx[(h, qtl)]
                        nk = c["wk"] // 128
                        for j in range(nk):
                            mk2 = em.instr("tensor", "transpose",
                                           ps_tr[:, s_ * 1024 + j * 128 :
                                                 s_ * 1024 + (j + 1) * 128],
                                           attn2_sb[:, s_ * 640 + j * 128 :
                                                    s_ * 640 + (j + 1) * 128],
                                           ident_sb[:],
                                           waits=[c["norm"], trev_mk[s_]] if j == 0 else (),
                                           inc="pe" if j == nk - 1 else None)
                        tr_mk[s_] = c["tr"] = mk2

                    def st2_copy(h, qtl):
                        s_ = h % 2
                        c = st_ctx[(h, qtl)]
                        em.instr("vector", "tensor_copy",
                                 attnT_sb[:, s_ * 640 : s_ * 640 + c["wk"]],
                                 ps_tr[:, s_ * 1024 : s_ * 1024 + c["wk"]],
                                 waits=[c["tr"], av_mk[s_]], inc="dv")
                        trev_mk[s_] = c["copy"] = em.last("dv")

                    def st2_av(h, qtl):
                        s_ = h % 2
                        c = st_ctx[(h, qtl)]
                        nk = c["wk"] // 128
                        ppav, wvav, slotav = pool.alloc(1)
                        kvh = h // 4
                        for j in range(nk):
                            kt = c["lo"] // 128 + j
                            mk3 = em.instr(
                                "tensor", "matmul", ppav[:, 0:128],
                                vtm_sb[:, (kvh * NDC + kt) * 128 :
                                       (kvh * NDC + kt + 1) * 128],
                                attnT_sb[:, s_ * 640 + j * 128 :
                                         s_ * 640 + (j + 1) * 128],
                                start=(j == 0), stop=(j == nk - 1),
                                waits=([c["copy"]] + wvav) if j == 0 else (),
                                inc="pe" if j == nk - 1 else None)
                        av_mk[s_] = c["av"] = mk3
                        c["avslot"] = slotav

                    def st2_evac(h, qtl):
                        s_ = h % 2
                        c = st_ctx.pop((h, qtl))
                        em.instr("vector", "tensor_copy",
                                 attnout_sb[:, h * TSL + qtl * 128 :
                                            h * TSL + (qtl + 1) * 128],
                                 ppav_of(c), waits=[c["av"]], inc="dv")
                        pool.add_reader(c["avslot"], em.last("dv"))
                        ao_last[0] = em.last("dv")

                    def ppav_of(c):
                        start, nb = c["avslot"]
                        return psum[:, start * 512 : start * 512 + 128]

                    exp_mk = [None, None]
                    tr_mk = [None, None]
                    av_mk = [None, None]
                    rs_war = [None, None]

                    def proj_rope_q(pair_, s_):
                        h = 2 * pair_ + s_
                        qs = (pair_ % 2) * 2 + s_
                        ppq, slotq, mmq, iq = proj_1024(8 + h, [])
                        qr_mk[qs] = rope(ppq,
                                         qrot_sb[:, qs * TSL : (qs + 1) * TSL],
                                         mmq, war=[last_sc[qs]])
                        pool.add_reader(slotq, qr_mk[qs])
                        after_use(iq, qr_mk[qs])

                    for pair in range(NH // 2):
                        if pair == 0:
                            proj_rope_q(0, 0)
                            proj_rope_q(0, 1)
                        hA, hB = 2 * pair, 2 * pair + 1
                        emit_scores(hA, 0)
                        emit_scores(hB, 0)
                        st1_dve(hA, 0)
                        st1_dve(hB, 0)
                        for qtl in range(8):
                            if qtl == 3 and pair + 1 < NH // 2:
                                proj_rope_q(pair + 1, 0)
                            if qtl == 5 and pair + 1 < NH // 2:
                                proj_rope_q(pair + 1, 1)
                            if qtl >= 1:
                                st2_recip(hA, qtl - 1)
                                st2_recip(hB, qtl - 1)
                                st2_norm(hA, qtl - 1)
                                st2_norm(hB, qtl - 1)
                                st2_tr(hA, qtl - 1)
                                st2_tr(hB, qtl - 1)
                            st1_exp(hA, qtl)
                            st1_exp(hB, qtl)
                            if qtl + 1 < 8:
                                emit_scores(hA, qtl + 1)
                                emit_scores(hB, qtl + 1)
                                st1_dve(hA, qtl + 1)
                                st1_dve(hB, qtl + 1)
                            if qtl >= 1:
                                st2_copy(hA, qtl - 1)
                                st2_copy(hB, qtl - 1)
                                st2_av(hA, qtl - 1)
                                st2_av(hB, qtl - 1)
                                st2_evac(hA, qtl - 1)
                                st2_evac(hB, qtl - 1)
                        for hh in (hA, hB):
                            st2_recip(hh, 7)
                            st2_norm(hh, 7)
                            st2_tr(hh, 7)
                            st2_copy(hh, 7)
                            st2_av(hh, 7)
                            st2_evac(hh, 7)
                    ao_ok = ao_last[0]

                    for eo in range(NDC):
                        i, loaded = consume(24 + eo)
                        pp, wv, slot = pool.alloc(2)
                        for nh in range(2):
                            for ei in range(NH):
                                mk = em.instr(
                                    "tensor", "matmul",
                                    pp[:, nh * 512 : nh * 512 + 512],
                                    ws.slot_ap(i)[:, ei * 128 : (ei + 1) * 128],
                                    attnout_sb[:, ei * TSL + nh * 512 :
                                               ei * TSL + nh * 512 + 512],
                                    start=(ei == 0), stop=(ei == NH - 1),
                                    waits=([loaded, ao_ok] + wv)
                                    if (nh == 0 and ei == 0) else (),
                                    inc="pe" if (nh == 1 and ei == NH - 1) else None,
                                )
                        after_use(i, mk)
                        em.instr("vector", "tensor_add", xc(eo, sl), xc(eo, sl),
                                 pp[:, 0:TSL], waits=[mk], inc="dv")
                        pool.add_reader(slot, em.last("dv"))
                    res_mark = em.last("dv")
                    rope_done = res_mark
        return res_mark

    def emit_flush(x_ready):
        with ExitStack() as ph:
            P = ph.enter_context
            stage = P(nc.sbuf_tensor("fl_stage", [128, 2 * TSL], F32))
            dma_mk = [None] * 2
            k = 0
            for dc in range(NDC):
                for sl in range(NSL):
                    sslot = stage[:, (k % 2) * TSL : (k % 2 + 1) * TSL]
                    em.instr("vector", "tensor_copy", sslot, xc(dc, sl),
                             waits=[x_ready, dma_mk[k % 2]], inc="dv")
                    osem = "oA" if k % 2 == 0 else "oB"
                    em.instr("sync", "dma_start",
                             y[dc, :, sl * TSL : (sl + 1) * TSL], sslot,
                             waits=[em.last("dv")], inc=osem, incn=16)
                    dma_mk[k % 2] = em.last(osem)
                    k += 1
            em.wait("sync", em.last("oA"), em.last("oB"))

    def phase_gate():
        snap = [em.last(s) for s in ("pe", "dv", "ac")]
        for e in ("gpsimd", "sync", "tensor", "vector", "scalar"):
            em.wait(e, *snap)

    mark = emit_attn_a(x_loaded)
    if n_phases >= 2:
        phase_gate()
        mark = emit_ffn(mark, final=False, tag="f1")
    if n_phases >= 3:
        phase_gate()
        mark = emit_attn_b(mark)
    if n_phases >= 4:
        phase_gate()
        emit_ffn(mark, final=True, tag="f2")
    else:
        phase_gate()
        emit_flush(mark)

    with nc.Block() as block:
        @block.sync
        def _(e):
            for fn in em.ops["sync"]:
                fn(e)

        @block.gpsimd
        def _(e):
            for fn in em.ops["gpsimd"]:
                fn(e)

        @block.tensor
        def _(e):
            for fn in em.ops["tensor"]:
                fn(e)

        @block.vector
        def _(e):
            for fn in em.ops["vector"]:
                fn(e)

        @block.scalar
        def _(e):
            for fn in em.ops["scalar"]:
                fn(e)

    st.close()
    return nc


def _prep_shared(ins):
    f32 = np.float32
    out = {}
    t44 = lambda a, r, c: np.ascontiguousarray(
        a.reshape(r, 128, c, 128).transpose(2, 0, 1, 3))
    out["wqa"] = t44(np.ascontiguousarray(ins["L0_attn_a_Wq"].T), NDC, NDC)
    out["wka"] = t44(np.ascontiguousarray(ins["L0_attn_a_Wk"].T), NDC, NDC)
    out["wva"] = t44(np.ascontiguousarray(ins["L0_attn_a_Wv"].T), NDC, NDC)
    out["woa"] = t44(np.ascontiguousarray(ins["L0_attn_a_Wo"].T), NDC, NDC)
    cols = lambda v, n: np.ascontiguousarray(v.reshape(n, 128).T, dtype=f32)
    out["biases"] = np.concatenate(
        [cols(ins["L0_attn_a_b" + k], 16) for k in "qkvo"], axis=1)
    out["wg"] = t44(np.ascontiguousarray(ins["L0_ffn_wq_gate"].T), NDC, NHC)
    out["wu"] = t44(np.ascontiguousarray(ins["L0_ffn_wq_up"].T), NDC, NHC)
    out["wd"] = t44(np.ascontiguousarray(ins["L0_ffn_wq_down"].T), NHC, NDC)
    out["sgu"] = np.concatenate(
        [cols(ins["L0_ffn_s_gate"], NHC), cols(ins["L0_ffn_s_up"], NHC)], axis=1)
    out["sdn"] = cols(ins["L0_ffn_s_down"], NDC)
    perm = np.concatenate([np.arange(0, 128, 2), np.arange(1, 128, 2)])
    qp = ins["L0_attn_b_wq"].reshape(D, NH, 128)[:, :, perm]
    kp = ins["L0_attn_b_wk"].reshape(D, NKV, 128)[:, :, perm]
    vp = ins["L0_attn_b_wv"].reshape(D, NKV, 128)
    out["wqb"] = np.ascontiguousarray(
        qp.transpose(1, 0, 2).reshape(NH, NDC, 128, 128))
    out["wkb"] = np.ascontiguousarray(
        kp.transpose(1, 0, 2).reshape(NKV, NDC, 128, 128))
    out["wvb"] = np.ascontiguousarray(
        vp.transpose(1, 0, 2).reshape(NKV, NDC, 128, 128))
    out["wob"] = t44(ins["L0_attn_b_wo"], NDC, NH)
    out["lnw"] = cols(ins["L0_attn_b_ln_w"], NDC)
    out["ones_c"] = np.ones((128, 128), dtype=ml_dtypes.bfloat16)
    out["ident_c"] = np.eye(128, dtype=ml_dtypes.bfloat16)
    out["neg128_c"] = np.full((1, 128), -128.0, dtype=ml_dtypes.bfloat16)
    qi = np.arange(128)[:, None]
    maske = np.zeros((128, 1280), dtype=f32)
    off = 0
    for qt in range(4):
        w = 128 * (qt + 1)
        kj = np.arange(w)[None, :]
        maske[:, off : off + w] = np.where(kj <= qi + 128 * qt, 0.0, -1e30)
        off += w
    out["maske_c"] = maske.astype(ml_dtypes.bfloat16)
    kj = np.arange(640)[None, :]
    out["maskb_c"] = np.where((kj >= qi) & (kj <= qi + WIN), 0.0, -1e30).astype(ml_dtypes.bfloat16)
    invf = 1.0 / 10000.0 ** (np.arange(0, HD, 2, dtype=np.float64) / HD)
    ang = np.arange(S, dtype=np.float64)[:, None] * invf[None, :]
    cosT = np.cos(ang).T.astype(ml_dtypes.bfloat16)
    sinT = np.sin(ang).T.astype(ml_dtypes.bfloat16)
    out["cos2_c"] = np.ascontiguousarray(np.concatenate([cosT, cosT], axis=0))
    out["sin2_c"] = np.ascontiguousarray(np.concatenate([sinT, -sinT], axis=0))
    return out


def run(inputs, n_phases=4, trace=False):
    shared = _prep_shared(inputs)
    x = np.asarray(inputs["L0_attn_a_x"], dtype=np.float32)
    B = x.shape[0]
    in_maps = []
    for b in range(B):
        m = dict(shared)
        m["xT"] = np.ascontiguousarray(x[b].T).reshape(NDC, 128, S)
        in_maps.append(m)
    nc = build_program(n_phases=n_phases)
    res = run_bass_kernel_spmd(nc, in_maps, core_ids=list(range(8)), trace=trace)
    outs = []
    for b in range(B):
        yb = res.results[b]["y"]
        outs.append(yb.reshape(D, S).T)
    return np.stack(outs, axis=0), res


def kernel(**inputs):
    out, _ = run(inputs, n_phases=4)
    return out.astype(np.float32)
